# version 02 (final)
# baseline (speedup 1.0000x reference)
import numpy as np
from contextlib import ExitStack
import concourse.bass as bass
import concourse.mybir as mybir
from concourse.bass_utils import run_bass_kernel_spmd

F32 = mybir.dt.float32
BF16 = mybir.dt.bfloat16
AF = mybir.ActivationFunctionType
ALU = mybir.AluOpType

P = 128
DM = 1024
T = 2048
NCH = DM // P
TB = 512
NTB = T // TB
NTT = T // P

ENGS = ("sq", "pe", "ve", "se", "gp")


_BUFN = [0]


def ngroup(n):
    def wrap(f):
        f.n_insts = n
        return f
    return wrap


class Sched:
    def __init__(self, nc, ctx):
        self.nc = nc
        self.sem = {e: ctx.enter_context(nc.semaphore("sem_" + e)) for e in ENGS}
        self.count = {e: 0 for e in ENGS}
        self.queue = {e: [] for e in ENGS}
        self.last_writer = {}
        self.readers = {}
        self.waited = {e: {f: 0 for f in ENGS} for e in ENGS}

    def op(self, eng, emit, reads=(), writes=(), dma=False):
        waits = {}

        def need(dep):
            if dep is None:
                return
            deng, dcount = dep
            if deng == eng and eng == "pe":
                return
            if self.waited[eng].get(deng, 0) >= dcount:
                return
            waits[deng] = max(waits.get(deng, 0), dcount)

        for r in reads:
            need(self.last_writer.get(r))
        for w in writes:
            need(self.last_writer.get(w))
            for rd in self.readers.get(w, ()):
                need(rd)
        for deng, dcount in waits.items():
            self.waited[eng][deng] = dcount
        k = 16 if dma else 1
        tick_each = dma
        n = getattr(emit, "n_insts", 1)
        self.queue[eng].append((dict(waits), emit, k, tick_each))
        self.count[eng] += k * (n if tick_each else 1)
        after = self.count[eng]
        for r in reads:
            self.readers.setdefault(r, []).append((eng, after))
        for w in writes:
            self.last_writer[w] = (eng, after)
            self.readers[w] = []
        return (eng, after)

    def barrier(self):
        snap = dict(self.count)
        for eng in ENGS:
            waits = {f: snap[f] for f in ENGS if f != eng and snap[f] > 0}
            def emitnop(h):
                return h.nop()
            self.queue[eng].append((waits, emitnop, 1, False))
            self.count[eng] += 1
            for f, c in waits.items():
                self.waited[eng][f] = max(self.waited[eng][f], c)
            self.waited[eng][eng] = self.count[eng]

    def alias(self, newkey, oldkeys):
        rl = self.readers.setdefault(newkey, [])
        for ok in oldkeys:
            lw = self.last_writer.get(ok)
            if lw is not None:
                rl.append(lw)
            rl.extend(self.readers.get(ok, ()))

    def takeover(self, b, bufname, newkeys):
        old = b.live.get(bufname, [])
        for nk in newkeys:
            self.alias(nk, old)
        b.live[bufname] = list(newkeys)

    def build(self, block):
        secs = {"sq": block.sync, "pe": block.tensor, "ve": block.vector,
                "se": block.scalar, "gp": block.gpsimd}
        for eng in ENGS:
            def body(h, eng=eng):
                c = 0
                for waits, emit, k, each in self.queue[eng]:
                    for deng in sorted(waits):
                        h.wait_ge(self.sem[deng], waits[deng])
                    insts = emit(h)
                    if not isinstance(insts, (list, tuple)):
                        insts = [insts]
                    if each:
                        for i in insts:
                            i.then_inc(self.sem[eng], k)
                            c += k
                    else:
                        insts[-1].then_inc(self.sem[eng], k)
                        c += k
                assert c == self.count[eng], (eng, c, self.count[eng])
            secs[eng](body)


class B:
    def __init__(self):
        self.live = {}


def make_common_bufs(ctx, nc, sch):
    sfx = "_u%d" % _BUFN[0]; _BUFN[0] += 1
    b = B()
    A = lambda name, shape, dt: ctx.enter_context(nc.sbuf_tensor(name + sfx, shape, dt))
    b.X = A("X", [P, NCH * T], F32)
    b.BIGA = A("BIGA", [P, NCH * T], BF16)
    b.BIGB = A("BIGB", [P, NCH * T], BF16)
    b.POOL1 = A("POOL1", [P, NCH * DM], BF16)
    b.POOL2 = A("POOL2", [P, NCH * DM], BF16)
    b.ACTALL = A("ACTALL", [P, 8 * DM], BF16)
    b.WSTG = A("WSTG", [P, 2 * DM], F32)
    b.ROWS = A("ROWS", [P, 4 * TB], F32)
    b.TMPT = A("TMPT", [P, 2 * TB], F32)
    b.WR = A("WR", [P, NCH * 16], BF16)
    b.EYE = A("EYE", [P, P], F32)
    b.EYEB = A("EYEB", [P, P], BF16)
    b.ONES = A("ONES", [P, P + 8], F32)
    b.ONESB = A("ONESB", [P, 8], BF16)
    b.LOG = A("LOG", [P, T], F32)
    b.LT = A("LT", [P, 256], F32)
    b.M8 = A("M8", [P, 8], F32)
    b.CONSTS = A("CONSTS", [P, 8], F32)
    b.cidx = {}
    b.SGT = A("SGT", [P, 16], F32)
    b.G = A("G", [P, 256], F32)
    PA = lambda name, shape, dt=F32: ctx.enter_context(nc.psum_tensor(name + sfx, shape, dt))
    b.psA = PA("psA", [P, TB])
    b.psB = PA("psB", [P, TB])
    b.psY0 = PA("psY0", [P, 2 * TB])
    b.psY1 = PA("psY1", [P, 2 * TB])
    b.psTB = PA("psTB", [P, 2 * TB], BF16)
    b.XBT = b.POOL1
    b.SQT = b.POOL1
    b.WMH = b.POOL2
    b.KVB = b.POOL2
    b.WMG = b.POOL1
    return b


def reg_consts(b, vals):
    for v in vals:
        if v not in b.cidx:
            assert len(b.cidx) < 8
            b.cidx[v] = len(b.cidx)


def cb(b, v, p=P):
    return b.CONSTS[0:p, b.cidx[v]:b.cidx[v] + 1]


def emit_setup_consts(sch, b, d):
    def ld_eye(sq):
        return sq.dma_start(out=b.EYE[:, :], in_=d["eye"])
    sch.op("sq", ld_eye, writes=["EYE"], dma=True)

    def mk(ve):
        ve.memset(b.ONES[0:1, 0:P], 1.0)
        return ve.memset(b.ONESB[:, 0:1], 1.0)
    sch.op("ve", mk, writes=["ONES"])

    def mkc(ve):
        i = None
        for v, ix in b.cidx.items():
            i = ve.memset(b.CONSTS[:, ix:ix + 1], v)
        return i
    if b.cidx:
        sch.op("ve", mkc, writes=["CONSTS"])

    def mk2(ve):
        return ve.tensor_copy(b.EYEB[:, :], b.EYE[:, :])
    sch.op("ve", mk2, reads=["EYE"], writes=["EYEB"])


def emit_load_x(sch, b, d, key="X"):
    @ngroup(NCH)
    def ld(sq):
        return [sq.dma_start(out=b.X[:, T * ch: T * (ch + 1)], in_=d["xT"][P * ch:P * (ch + 1), :])
                for ch in range(NCH)]
    sch.op("sq", ld, writes=[key], dma=True)


def emit_store_x(sch, b, d, xkey="X"):
    @ngroup(NCH)
    def st(sq):
        return [sq.dma_start(out=d["yT"][P * ch:P * (ch + 1), :], in_=b.X[:, T * ch: T * (ch + 1)])
                for ch in range(NCH)]
    sch.op("sq", st, reads=[xkey], writes=["OUT"], dma=True)

    def fin(sq):
        return sq.nop()
    sch.op("sq", fin, reads=["OUT"])


def emit_load_weight_1024(sch, b, d, dname, dest, destkey, warkeys=()):
    W = d[dname].shape[1]
    for ch in range(NCH):
        hp = ch % 2
        def ld(sq, ch=ch, hp=hp):
            return sq.dma_start(out=b.WSTG[:, DM * hp:DM * hp + W],
                                in_=d[dname][P * ch:P * (ch + 1), :])
        sch.op("sq", ld, writes=[("WSTG", hp)], dma=True)

        def cast(gp, ch=ch, hp=hp):
            return gp.tensor_copy(dest[:, W * ch: W * (ch + 1)], b.WSTG[:, DM * hp:DM * hp + W])
        sch.op("gp", cast, reads=[("WSTG", hp)], writes=[(destkey, ch)])



def emit_ln(sch, b, X, XKEY, XH, XHKEY, tag, xbt_war=()):
    for tb in range(NTB):
        xbk, sqk = "XBT", "SQT"

        def cast(se, tb=tb):
            out = None
            for ch in range(NCH):
                xs = X[:, T * ch + TB * tb: T * ch + TB * (tb + 1)]
                se.activation(out=b.XBT[:, TB * ch:TB * (ch + 1)], in_=xs, func=AF.Identity)
                out = se.activation(out=b.SQT[:, 4096 + TB * ch:4096 + TB * (ch + 1)], in_=xs,
                                    func=AF.Square)
            return out
        sch.op("se", cast, reads=[XKEY], writes=[xbk, sqk])

        def stats(pe, tb=tb):
            for ch in range(NCH):
                pe.matmul(b.psA[0:1, :], b.ONESB[:, 0:1], b.XBT[:, TB * ch:TB * (ch + 1)],
                          start=(ch == 0), stop=(ch == NCH - 1))
            i = None
            for ch in range(NCH):
                i = pe.matmul(b.psB[0:1, :], b.ONESB[:, 0:1], b.SQT[:, 4096 + TB * ch:4096 + TB * (ch + 1)],
                              start=(ch == 0), stop=(ch == NCH - 1))
            return i
        sch.op("pe", stats, reads=[xbk, sqk, "ONES"], writes=["psA", "psB"])

        R = b.ROWS
        sch.op("se", lambda se: se.activation(out=R[0:1, 0:TB], in_=b.psA[0:1, :],
                                              func=AF.Identity, scale=1.0 / DM),
               reads=["psA"], writes=[("rowm",)])
        sch.op("se", lambda se: se.activation(out=R[0:1, 2 * TB:3 * TB], in_=R[0:1, 0:TB], func=AF.Square),
               reads=[("rowm",)], writes=[("rowt",)])
        sch.op("ve", lambda ve: ve.scalar_tensor_tensor(R[0:1, 2 * TB:3 * TB], b.psB[0:1, :], 1.0 / DM,
                                                        R[0:1, 2 * TB:3 * TB], ALU.mult, ALU.subtract),
               reads=["psB", ("rowt",)], writes=[("rowt",)])
        sch.op("se", lambda se: se.activation(out=R[0:1, 2 * TB:3 * TB], in_=R[0:1, 2 * TB:3 * TB],
                                              func=AF.Ln, bias=cb(b, 1e-5, 1)),
               reads=[("rowt",), "CONSTS"], writes=[("rowt",)])
        sch.op("se", lambda se: se.activation(out=R[0:1, TB:2 * TB], in_=R[0:1, 2 * TB:3 * TB],
                                              func=AF.Exp, scale=-0.5),
               reads=[("rowt",)], writes=[("rowA",)])
        sch.op("ve", lambda ve: ve.scalar_tensor_tensor(R[0:1, 2 * TB:3 * TB], R[0:1, 0:TB], -1.0,
                                                        R[0:1, TB:2 * TB], ALU.mult, ALU.mult),
               reads=[("rowm",), ("rowA",)], writes=[("rowB",)])

        def bcast(pe):
            pe.matmul(b.psY0[:, 0:TB], b.ONES[0:1, 0:P], R[0:1, TB:2 * TB], start=True, stop=True)
            return pe.matmul(b.psY0[:, TB:2 * TB], b.ONES[0:1, 0:P], R[0:1, 2 * TB:3 * TB],
                             start=True, stop=True)
        sch.op("pe", bcast, reads=[("rowA",), ("rowB",), "ONES"], writes=["psY0"])

        for ch in range(NCH):
            tp = ch % 2
            def ap1(ve, tb=tb, ch=ch, tp=tp):
                return ve.tensor_mul(b.TMPT[:, TB * tp:TB * (tp + 1)],
                                     X[:, T * ch + TB * tb: T * ch + TB * (tb + 1)], b.psY0[:, 0:TB])
            sch.op("ve", ap1, reads=[XKEY, "psY0"], writes=[("tmp", tp)])

            def ap2(ve, tb=tb, ch=ch, tp=tp):
                return ve.tensor_add(XH[:, T * ch + TB * tb: T * ch + TB * (tb + 1)],
                                     b.TMPT[:, TB * tp:TB * (tp + 1)], b.psY0[:, TB:2 * TB])
            sch.op("ve", ap2, reads=[("tmp", tp), "psY0"], writes=[(XHKEY, tb)])



def emit_l0ffn(sch, b, d, tag):
    X, XKEY = b.X, "X"
    XH, QT = b.BIGA, b.BIGB
    XHKEY = "XH"

    wmhk = [("WMH", c) for c in range(NCH)]
    sch.takeover(b, "POOL2", wmhk)
    emit_load_weight_1024(sch, b, d, "wmhT", b.WMH, "WMH")

    def ld_wr(sq):
        return sq.dma_start(out=b.WSTG[:, 0:P].rearrange("p (c e) -> p c e", c=NCH),
                            in_=d["wrT"].rearrange("(c p) e -> p c e", p=P))
    sch.op("sq", ld_wr, writes=[("WSTG", 0)], dma=True)
    sch.op("gp", lambda gp: gp.tensor_copy(b.WR[:, :], b.WSTG[:, 0:P]),
           reads=[("WSTG", 0)], writes=["WR"])

    sch.takeover(b, "BIGA", [(XHKEY, tb) for tb in range(NTB)])
    sch.takeover(b, "POOL1", ["XBT", "SQT"])
    emit_ln(sch, b, X, XKEY, XH, XHKEY, tag)
    xhk = [(XHKEY, tb) for tb in range(NTB)]

    for tb in range(NTB):
        def rt(pe, tb=tb):
            i = None
            for ch in range(NCH):
                i = pe.matmul(b.psA[0:16, :], b.WR[:, 16 * ch:16 * (ch + 1)],
                              XH[:, T * ch + TB * tb: T * ch + TB * (tb + 1)],
                              start=(ch == 0), stop=(ch == NCH - 1))
            return i
        sch.op("pe", rt, reads=[(XHKEY, tb), "WR"], writes=["psA"])
        def rte(se, tb=tb):
            return se.activation(out=b.LOG[0:16, TB * tb:TB * (tb + 1)], in_=b.psA[0:16, :],
                                 func=AF.Identity)
        sch.op("se", rte, reads=["psA"], writes=[("LOG", tb)])

    sch.takeover(b, "BIGB", qtk_pre := [("QT", h, tb) for h in range(8) for tb in range(NTB)])
    for h in range(8):
        for tb in range(NTB):
            def qm(pe, h=h, tb=tb):
                i = None
                for ch in range(NCH):
                    i = pe.matmul(b.psB[:, :], b.WMH[:, DM * ch + P * h: DM * ch + P * h + P],
                                  XH[:, T * ch + TB * tb: T * ch + TB * (tb + 1)],
                                  start=(ch == 0), stop=(ch == NCH - 1))
                return i
            sch.op("pe", qm, reads=[(XHKEY, tb)] + wmhk, writes=["psB"])
            def qe(se, h=h, tb=tb):
                return se.activation(out=QT[:, T * h + TB * tb: T * h + TB * (tb + 1)],
                                     in_=b.psB[:, :], func=AF.Identity)
            sch.op("se", qe, reads=["psB"], writes=[("QT", h, tb)])
    qtk = [("QT", h, tb) for h in range(8) for tb in range(NTB)]

    wmgk = [("WMG", c) for c in range(NCH)]
    sch.takeover(b, "POOL1", wmgk)
    emit_load_weight_1024(sch, b, d, "wmgT", b.WMG, "WMG")

    for tt in range(NTT):
        def ltr(pe, tt=tt):
            return pe.transpose(b.psY1[:, 0:P], b.LOG[:, P * tt:P * (tt + 1)], b.EYE[:, :])
        sch.op("pe", ltr, reads=[("LOG", tb) for tb in range(NTB)] + ["EYE"], writes=["psY1"])
        def lte(ve, tt=tt):
            return ve.tensor_copy(b.LT[:, 16 * tt:16 * (tt + 1)], b.psY1[:, 0:16])
        sch.op("ve", lte, reads=["psY1"], writes=[("LT", tt)])

    for tt in range(NTT):
        def g1(ve, tt=tt):
            return ve.max(out=b.M8[:, :], in_=b.LT[:, 16 * tt:16 * (tt + 1)])
        sch.op("ve", g1, reads=[("LT", tt)], writes=[("M8",)])
        def g2(ve, tt=tt):
            return ve.tensor_scalar(b.G[:, 16 * tt:16 * (tt + 1)], b.LT[:, 16 * tt:16 * (tt + 1)],
                                    b.M8[:, 1:2], None, ALU.is_ge)
        sch.op("ve", g2, reads=[("LT", tt), ("M8",)], writes=[("G", tt)])
        def g3(se, tt=tt):
            return se.activation(out=b.SGT[:, :], in_=b.LT[:, 16 * tt:16 * (tt + 1)],
                                 func=AF.Sigmoid)
        sch.op("se", g3, reads=[("LT", tt)], writes=[("SG",)])
        def g4(ve, tt=tt):
            return ve.tensor_mul(b.G[:, 16 * tt:16 * (tt + 1)], b.G[:, 16 * tt:16 * (tt + 1)],
                                 b.SGT[:, :])
        sch.op("ve", g4, reads=[("G", tt), ("SG",)], writes=[("G", tt)])

    ACC = XH
    def zacc(gp):
        return gp.memset(ACC[:, :], 0.0)
    acck = [("ACC", tt) for tt in range(NTT)]
    sch.takeover(b, "BIGA", acck)
    sch.op("gp", zacc, writes=acck)

    def kv_stream(e, slot):
        for h in range(8):
            hp = h % 2
            @ngroup(2)
            def ldkv(sq, e=e, h=h, hp=hp):
                i1 = sq.dma_start(out=b.WSTG[:, TB * hp:TB * hp + 256], in_=d["kfT"][h, e, :, :])
                i2 = sq.dma_start(out=b.WSTG[:, TB * hp + 256:TB * hp + TB].rearrange("p (c e) -> p c e", c=2),
                                  in_=d["vf"][h, e, :, :].rearrange("(c p) e -> p c e", p=P))
                return [i1, i2]
            sch.op("sq", ldkv, writes=[("WSTG", hp)], dma=True)

            def cast(gp, e=e, h=h, hp=hp, slot=slot):
                return gp.tensor_copy(b.KVB[:, 4096 * slot + TB * h: 4096 * slot + TB * (h + 1)],
                                      b.WSTG[:, TB * hp:TB * (hp + 1)])
            sch.op("gp", cast, reads=[("WSTG", hp)], writes=[("KVB", slot, h)])

    kvbk = [("KVB", s, h) for s in range(2) for h in range(8)]
    sch.takeover(b, "POOL2", kvbk)
    kv_stream(0, 0)
    for e in range(16):
        slot = e % 2
        if e < 15:
            kv_stream(e + 1, 1 - slot)
        for tb in range(NTB):
            for h in range(8):
                for bb in range(2):
                    idx = (h * 2 + bb) % 2
                    pss, psk = (b.psA, "psA") if idx == 0 else (b.psB, "psB")
                    def sc(pe, e=e, tb=tb, h=h, bb=bb, slot=slot, pss=pss):
                        return pe.matmul(pss[:, :],
                                         b.KVB[:, 4096 * slot + TB * h + P * bb: 4096 * slot + TB * h + P * (bb + 1)],
                                         QT[:, T * h + TB * tb: T * h + TB * (tb + 1)],
                                         start=True, stop=True)
                    sch.op("pe", sc, reads=[("KVB", slot, h), ("QT", h, tb)], writes=[psk])
                    def gl(se, e=e, tb=tb, h=h, bb=bb, pss=pss):
                        return se.activation(out=b.ACTALL[:, DM * h + TB * bb: DM * h + TB * (bb + 1)],
                                             in_=pss[:, :], func=AF.Gelu)
                    sch.op("se", gl, reads=[psk], writes=[("ACT", h, bb)])
            for j in range(4):
                tt = tb * 4 + j
                psy, pyk = (b.psY0, "psY0") if j % 2 == 0 else (b.psY1, "psY1")
                def pv(pe, e=e, tb=tb, j=j, slot=slot, psy=psy):
                    i = None
                    for h in range(8):
                        for bb in range(2):
                            i = pe.matmul(psy[:, P * h:P * (h + 1)],
                                          b.ACTALL[:, DM * h + TB * bb + P * j: DM * h + TB * bb + P * (j + 1)],
                                          b.KVB[:, 4096 * slot + TB * h + 256 + P * bb: 4096 * slot + TB * h + 256 + P * (bb + 1)],
                                          start=(bb == 0), stop=(bb == 1))
                    return i
                sch.op("pe", pv,
                       reads=[("ACT", h, bb) for h in range(8) for bb in range(2)] +
                             [("KVB", slot, h) for h in range(8)],
                       writes=[pyk])
                def gate(ve, e=e, tt=tt, psy=psy):
                    return ve.scalar_tensor_tensor(ACC[:, DM * tt:DM * (tt + 1)], psy[:, :],
                                                   b.G[:, 16 * tt + e:16 * tt + e + 1],
                                                   ACC[:, DM * tt:DM * (tt + 1)], ALU.mult, ALU.add)
                sch.op("ve", gate, reads=[pyk, ("G", tt), ("ACC", tt)],
                       writes=[("ACC", tt)])

    YT = QT
    ytk = [("YT", tt) for tt in range(NTT)]
    sch.takeover(b, "BIGB", ytk)
    for tt in range(NTT):
        def tr(pe, tt=tt):
            i = None
            for dch in range(NCH):
                i = pe.transpose(b.psTB[:, P * dch:P * (dch + 1)],
                                 ACC[:, DM * tt + P * dch: DM * tt + P * (dch + 1)], b.EYEB[:, :])
            return i
        sch.op("pe", tr, reads=[("ACC", tt), "EYEB"], writes=["psTB"])

        def tre(ve, tt=tt):
            return ve.tensor_copy(YT.ap().rearrange("p (c t) -> p c t", c=NCH)[:, :, P * tt:P * (tt + 1)],
                                  b.psTB.ap().rearrange("p (c t) -> p c t", c=NCH))
        sch.op("ve", tre, reads=["psTB"], writes=[("YT", tt)])

    for dch in range(NCH):
        for tb in range(NTB):
            pss, psk = (b.psA, "psA") if (dch * NTB + tb) % 2 == 0 else (b.psB, "psB")
            def mg(pe, dch=dch, tb=tb, pss=pss):
                i = None
                for ch in range(NCH):
                    i = pe.matmul(pss[:, :], b.WMG[:, DM * ch + P * dch: DM * ch + P * dch + P],
                                  YT[:, T * ch + TB * tb: T * ch + TB * (tb + 1)],
                                  start=(ch == 0), stop=(ch == NCH - 1))
                return i
            sch.op("pe", mg, reads=[("YT", tb * 4 + j) for j in range(4)] + wmgk, writes=[psk])

            def res(ve, dch=dch, tb=tb, pss=pss):
                return ve.scalar_tensor_tensor(X[:, T * dch + TB * tb: T * dch + TB * (tb + 1)],
                                               pss[:, :], 0.5,
                                               X[:, T * dch + TB * tb: T * dch + TB * (tb + 1)],
                                               ALU.mult, ALU.add)
            sch.op("ve", res, reads=[psk, XKEY], writes=[XKEY])



def build_l0ffn_launch():
    nc = bass.Bass("TRN2", target_bir_lowering=False, debug=False)
    d = {}
    d["xT"] = nc.dram_tensor("xT", [DM, T], F32, kind="ExternalInput").ap()
    d["wrT"] = nc.dram_tensor("wrT", [DM, 16], F32, kind="ExternalInput").ap()
    d["wmhT"] = nc.dram_tensor("wmhT", [DM, DM], F32, kind="ExternalInput").ap()
    d["wmgT"] = nc.dram_tensor("wmgT", [DM, DM], F32, kind="ExternalInput").ap()
    d["kfT"] = nc.dram_tensor("kfT", [8, 16, P, 256], F32, kind="ExternalInput").ap()
    d["vf"] = nc.dram_tensor("vf", [8, 16, 256, P], F32, kind="ExternalInput").ap()
    d["eye"] = nc.dram_tensor("eye", [P, P], F32, kind="ExternalInput").ap()
    d["yT"] = nc.dram_tensor("yT", [DM, T], F32, kind="ExternalOutput").ap()
    with ExitStack() as ctx:
        sch = Sched(nc, ctx)
        b = make_common_bufs(ctx, nc, sch)
        reg_consts(b, [1e-5, -10.0])
        emit_setup_consts(sch, b, d)
        emit_load_x(sch, b, d)
        emit_l0ffn(sch, b, d, "f0")
        emit_store_x(sch, b, d)
        with nc.Block() as block:
            sch.build(block)
    return nc


def host_inputs_l0ffn(inp, core):
    s, half = core // 2, core % 2
    x = np.asarray(inp["L0_attn_query"])
    xs = x[s, half * T:(half + 1) * T, :]
    C = np.ascontiguousarray
    return {
        "xT": C(xs.T),
        "wrT": C(np.asarray(inp["L0_ffn_w_router"]).T),
        "wmhT": C(np.asarray(inp["L0_ffn_w_mh"]).T),
        "wmgT": C(np.asarray(inp["L0_ffn_w_mg"]).T),
        "kfT": C(np.asarray(inp["L0_ffn_k_ffwd"]).transpose(0, 1, 3, 2)),
        "vf": C(np.asarray(inp["L0_ffn_v_ffwd"])),
        "eye": np.eye(P, dtype=np.float32),
    }



def emit_ln_tile(sch, b, XS, xsk, dst, dstk, psA="psA", psB="psB", psR=None):
    if psR is None:
        psR = "psY0" if hasattr(b, "psY0") else None
    if psR is not None:
        pA, pB, pR = getattr(b, psA), getattr(b, psB), getattr(b, psR)
        pRa, pRb, ka, kb2 = pR[:, 0:TB], pR[:, TB:2 * TB], psR, psR
    else:
        pA, pB = getattr(b, psA), getattr(b, psB)
        pRa, pRb, ka, kb2 = b.pSa[:, 0:TB], b.pSb[:, 0:TB], "pSa", "pSb"

    def cast(se):
        out = None
        for ch in range(NCH):
            xs = XS[:, TB * ch:TB * (ch + 1)]
            se.activation(out=b.XBT[:, TB * ch:TB * (ch + 1)], in_=xs, func=AF.Identity)
            out = se.activation(out=b.SQT[:, 4096 + TB * ch:4096 + TB * (ch + 1)], in_=xs,
                                func=AF.Square)
        return out
    sch.op("se", cast, reads=[xsk], writes=["XBT", "SQT"])

    def stats(pe):
        for ch in range(NCH):
            pe.matmul(pA[0:1, :], b.ONESB[:, 0:1], b.XBT[:, TB * ch:TB * (ch + 1)],
                      start=(ch == 0), stop=(ch == NCH - 1))
        i = None
        for ch in range(NCH):
            i = pe.matmul(pB[0:1, :], b.ONESB[:, 0:1], b.SQT[:, 4096 + TB * ch:4096 + TB * (ch + 1)],
                          start=(ch == 0), stop=(ch == NCH - 1))
        return i
    sch.op("pe", stats, reads=["XBT", "SQT", "ONES"], writes=[psA, psB])

    R = b.ROWS
    sch.op("se", lambda se: se.activation(out=R[0:1, 0:TB], in_=pA[0:1, :], func=AF.Identity,
                                          scale=1.0 / DM),
           reads=[psA], writes=[("rowm",)])
    sch.op("se", lambda se: se.activation(out=R[0:1, 2 * TB:3 * TB], in_=R[0:1, 0:TB], func=AF.Square),
           reads=[("rowm",)], writes=[("rowt",)])
    sch.op("ve", lambda ve: ve.scalar_tensor_tensor(R[0:1, 2 * TB:3 * TB], pB[0:1, :], 1.0 / DM,
                                                    R[0:1, 2 * TB:3 * TB], ALU.mult, ALU.subtract),
           reads=[psB, ("rowt",)], writes=[("rowt",)])
    sch.op("se", lambda se: se.activation(out=R[0:1, 2 * TB:3 * TB], in_=R[0:1, 2 * TB:3 * TB],
                                          func=AF.Ln, bias=cb(b, 1e-5, 1)),
           reads=[("rowt",), "CONSTS"], writes=[("rowt",)])
    sch.op("se", lambda se: se.activation(out=R[0:1, TB:2 * TB], in_=R[0:1, 2 * TB:3 * TB],
                                          func=AF.Exp, scale=-0.5),
           reads=[("rowt",)], writes=[("rowA",)])
    sch.op("ve", lambda ve: ve.scalar_tensor_tensor(R[0:1, 2 * TB:3 * TB], R[0:1, 0:TB], -1.0,
                                                    R[0:1, TB:2 * TB], ALU.mult, ALU.mult),
           reads=[("rowm",), ("rowA",)], writes=[("rowB",), ("rowt",)])

    def bcast(pe):
        pe.matmul(pRa, b.ONES[0:1, 0:P], R[0:1, TB:2 * TB], start=True, stop=True)
        return pe.matmul(pRb, b.ONES[0:1, 0:P], R[0:1, 2 * TB:3 * TB],
                         start=True, stop=True)
    sch.op("pe", bcast, reads=[("rowA",), ("rowB",), "ONES"], writes=[ka, kb2])

    for ch in range(NCH):
        tp = ch % 2
        def ap1(ve, ch=ch, tp=tp):
            return ve.tensor_mul(b.TMPT[:, TB * tp:TB * (tp + 1)], XS[:, TB * ch:TB * (ch + 1)],
                                 pRa)
        sch.op("ve", ap1, reads=[xsk, ka], writes=[("tmp", tp)])

        def ap2(ve, ch=ch, tp=tp):
            return ve.tensor_add(dst[:, TB * ch:TB * (ch + 1)], b.TMPT[:, TB * tp:TB * (tp + 1)],
                                 pRb)
        sch.op("ve", ap2, reads=[("tmp", tp), kb2], writes=[dstk])



def make_attn0_bufs(ctx, nc):
    sfx = "_u%d" % _BUFN[0]; _BUFN[0] += 1
    b = B()
    A = lambda name, shape, dt: ctx.enter_context(nc.sbuf_tensor(name + sfx, shape, dt))
    b.KT = A("KT", [P, NCH * 2 * T], BF16)
    b.OW = A("OW", [P, NCH * DM], BF16)
    b.VW = A("VW", [P, NCH * DM], BF16)
    b.XSTG = A("XSTG", [P, NCH * TB], F32)
    b.XH = A("XH", [P, NCH * TB], BF16)
    b.XBT = A("XBT", [P, NCH * DM], BF16)
    b.SQT = b.XBT
    b.QB = A("QB", [P, NCH * TB], BF16)
    b.CTX = A("CTX", [P, NCH * TB], BF16)
    b.PT = A("PT", [P, 4 * TB], BF16)
    b.VPS = A("VPS", [P, 4 * 1040], BF16)
    b.VST = A("VST", [P, 2 * 1040], BF16)
    b.WSTG = A("WSTG", [P, 2 * DM], F32)
    b.ROWS = A("ROWS", [P, 4 * TB], F32)
    b.TMPT = A("TMPT", [P, 2 * TB], F32)
    b.REP = A("REP", [P, TB], F32)
    b.VBR = A("VBR", [P, DM], F32)
    b.XR = A("XR", [P, 2 * TB], F32)
    b.YS = A("YS", [P, 2 * TB], F32)
    b.BIA = A("BIA", [P, 4 * NCH], F32)
    b.ONES = A("ONES", [P, P], F32)
    b.ONESB = A("ONESB", [P, 8], BF16)
    b.CONSTS = A("CONSTS", [P, 8], F32)
    b.cidx = {}
    PA = lambda name, shape, dt=F32: ctx.enter_context(nc.psum_tensor(name + sfx, shape, dt))
    b.psA = PA("psA", [P, TB]); b.psB = PA("psB", [P, TB])
    b.psY0 = PA("psY0", [P, 2 * TB])
    b.pSa = PA("pSa", [P, TB]); b.pSb = PA("pSb", [P, TB])
    b.pOa = PA("pOa", [P, TB]); b.pOb = PA("pOb", [P, TB])

    return b


def build_l0attn_launch():
    nc = bass.Bass("TRN2", target_bir_lowering=False, debug=False)
    d = {}
    DI = lambda n, s: nc.dram_tensor(n, s, F32, kind="ExternalInput").ap()
    d["xT"] = DI("xT", [DM, T])
    d["xTf"] = DI("xTf", [DM, 2 * T])
    for w in ("qwT", "kwT", "vwT", "owT"):
        d[w] = DI(w, [DM, DM])
    for w in ("qb", "kb", "vb", "ob"):
        d[w] = DI(w, [DM])
    d["yT"] = nc.dram_tensor("yT", [DM, T], F32, kind="ExternalOutput").ap()
    d["vscr"] = nc.dram_tensor("vscr", [32 * P, 1040], BF16, kind="ExternalOutput").ap()
    with ExitStack() as ctx:
        sch = Sched(nc, ctx)
        b = make_attn0_bufs(ctx, nc)
        reg_consts(b, [1e-5, -10.0])
        emit_setup_consts_noeye(sch, b)
        emit_l0attn(sch, b, d)
        with nc.Block() as block:
            sch.build(block)
    return nc


def emit_setup_consts_noeye(sch, b):
    def mk(ve):
        ve.memset(b.ONES[0:1, 0:P], 1.0)
        i = ve.memset(b.ONESB[:, 0:1], 1.0)
        for v, ix in b.cidx.items():
            i = ve.memset(b.CONSTS[:, ix:ix + 1], v)
        return i
    sch.op("ve", mk, writes=["ONES", "CONSTS"])


def load_w_generic(sch, b, d, dname, dest, key, nping=2):
    for ch in range(NCH):
        hp = ch % nping
        def ld(sq, ch=ch, hp=hp, dname=dname):
            return sq.dma_start(out=b.WSTG[:, DM * hp:DM * (hp + 1)],
                                in_=d[dname][P * ch:P * (ch + 1), :])
        sch.op("sq", ld, writes=[("WSTG", hp)], dma=True)

        def cast(gp, ch=ch, hp=hp, dest=dest, key=key):
            return gp.tensor_copy(dest[:, DM * ch:DM * (ch + 1)], b.WSTG[:, DM * hp:DM * (hp + 1)])
        sch.op("gp", cast, reads=[("WSTG", hp)], writes=[(key, ch)])


def emit_l0attn(sch, b, d, fused=False):
    NF = 2 * T

    if fused:
        for tb in range(NTB):
            @ngroup(NCH)
            def ldx0(sq, tb=tb):
                return [sq.dma_start(out=b.XSTG[:, TB * ch:TB * (ch + 1)],
                                     in_=d["xT"][P * ch:P * (ch + 1), TB * tb:TB * (tb + 1)])
                        for ch in range(NCH)]
            sch.op("sq", ldx0, writes=["XSTG"], dma=True)
            emit_ln_tile(sch, b, b.XSTG, "XSTG", b.XH, "XH")
            @ngroup(NCH)
            def stxh(sq, tb=tb):
                return [sq.dma_start(out=d["cinx"][P * ch:P * (ch + 1), TB * tb:TB * (tb + 1)],
                                     in_=b.XH[:, TB * ch:TB * (ch + 1)])
                        for ch in range(NCH)]
            sch.op("sq", stxh, reads=["XH"], writes=[("cinx", tb)], dma=True)

        def ccx(gp):
            return gp.collective_compute("AllGather", ALU.bypass,
                                         replica_groups=[[0, 1], [2, 3], [4, 5], [6, 7]],
                                         ins=[d["cinx"].opt()], outs=[d["coutx"].opt()])
        sch.op("gp", ccx, reads=[("cinx", tb) for tb in range(NTB)], writes=["coutx"])

    @ngroup(4)
    def ldb(sq):
        return [sq.dma_start(out=b.BIA[:, NCH * i:NCH * (i + 1)],
                             in_=d[w].rearrange("(c p) -> p c", p=P),
                             allow_slow_non_contiguous=True)
                for i, w in enumerate(("qb", "kb", "vb", "ob"))]
    sch.op("sq", ldb, writes=["BIA"], dma=True)
    def qb8(ve):
        return ve.tensor_scalar_mul(b.BIA[:, 0:NCH], b.BIA[:, 0:NCH], 0.125)
    sch.op("ve", qb8, reads=["BIA"], writes=["BIA"])

    def ldvbrow(sq):
        return sq.dma_start(out=b.ROWS[0:1, 0:DM], in_=d["vb"].rearrange("(o x) -> o x", o=1))
    sch.op("sq", ldvbrow, writes=[("rowm",), ("rowA",)], dma=True)
    def vbrep(pe):
        pe.matmul(b.psY0[:, 0:TB], b.ONES[0:1, 0:P], b.ROWS[0:1, 0:TB], start=True, stop=True)
        return pe.matmul(b.psY0[:, TB:2 * TB], b.ONES[0:1, 0:P], b.ROWS[0:1, TB:2 * TB],
                         start=True, stop=True)
    sch.op("pe", vbrep, reads=[("rowm",), ("rowA",), "ONES"], writes=["psY0"])
    def vbrepe(ve):
        return ve.tensor_copy(b.VBR[:, :], b.psY0[:, :])
    sch.op("ve", vbrepe, reads=["psY0"], writes=["VBR"])

    def vones(ve):
        return ve.memset(b.VST.ap().rearrange("p (t h e) -> p t h e", t=2, h=16)[:, :, :, 64:65], 1.0)
    sch.op("ve", vones, writes=["VSTones"])


    load_w_generic(sch, b, d, "kwT", b.OW, "W")
    load_w_generic(sch, b, d, "vwT", b.VW, "VW")
    kwk = [("W", c) for c in range(NCH)]
    vwk = [("VW", c) for c in range(NCH)]
    for tb in range(NF // TB):
        if fused:
            half, tloc = tb // 4, (tb % 4) * TB
            @ngroup(NCH)
            def ldxf(sq, half=half, tloc=tloc):
                return [sq.dma_start(out=b.XH[:, TB * ch:TB * (ch + 1)],
                                     in_=d["coutx"][DM * half + P * ch:DM * half + P * (ch + 1),
                                                    tloc:tloc + TB])
                        for ch in range(NCH)]
            sch.op("sq", ldxf, reads=["coutx"], writes=["XH"], dma=True)
        else:
            @ngroup(NCH)
            def ldxf(sq, tb=tb):
                return [sq.dma_start(out=b.XSTG[:, TB * ch:TB * (ch + 1)],
                                     in_=d["xTf"][P * ch:P * (ch + 1), TB * tb:TB * (tb + 1)])
                        for ch in range(NCH)]
            sch.op("sq", ldxf, writes=["XSTG"], dma=True)
            emit_ln_tile(sch, b, b.XSTG, "XSTG", b.XH, "XH")
        for dch in range(NCH):
            def km(pe, dch=dch):
                i = None
                for ch in range(NCH):
                    i = pe.matmul(b.psA[:, :], b.OW[:, DM * ch + P * dch: DM * ch + P * dch + P],
                                  b.XH[:, TB * ch:TB * (ch + 1)], start=(ch == 0), stop=(ch == NCH - 1))
                return i
            sch.op("pe", km, reads=["XH"] + kwk, writes=["psA"])
            def ke(se, dch=dch, tb=tb):
                return se.activation(out=b.KT[:, NF * dch + TB * tb: NF * dch + TB * (tb + 1)],
                                     in_=b.psA[:, :], func=AF.Identity,
                                     bias=b.BIA[:, NCH + dch:NCH + dch + 1])
            sch.op("se", ke, reads=["psA", "BIA"], writes=[("KT", dch, tb)])
        for j in range(4):
            tt = tb * 4 + j
            sp = tt % 2
            for bb in range(2):
                def vm(pe, j=j, bb=bb):
                    i = None
                    for ch in range(NCH):
                        i = pe.matmul(b.psB[:, :],
                                      b.XH[:, TB * ch + P * j: TB * ch + P * (j + 1)],
                                      b.VW[:, DM * ch + TB * bb: DM * ch + TB * (bb + 1)],
                                      start=(ch == 0), stop=(ch == NCH - 1))
                    return i
                sch.op("pe", vm, reads=["XH"] + vwk, writes=["psB"])
                def vev(ve, sp=sp, bb=bb):
                    return ve.tensor_add(
                        b.VST.ap().rearrange("p (t h e) -> p t h e", t=2, h=16)[:, sp, 8 * bb:8 * (bb + 1), 0:64],
                        b.psB.ap().rearrange("p (h e) -> p h e", h=8),
                        b.VBR[:, TB * bb:TB * (bb + 1)].rearrange("p (h e) -> p h e", h=8))
                sch.op("ve", vev, reads=["psB", "VBR"], writes=[("VST", sp, bb)])
            def stv(sq, tt=tt, sp=sp):
                return sq.dma_start(out=d["vscr"][P * tt:P * (tt + 1), :],
                                    in_=b.VST[:, 1040 * sp:1040 * (sp + 1)])
            sch.op("sq", stv, reads=[("VST", sp, 0), ("VST", sp, 1), "VSTones"],
                   writes=[("vscr", tt)], dma=True)

    load_w_generic(sch, b, d, "qwT", b.OW, "W")
    load_w_generic(sch, b, d, "owT", b.VW, "VW")
    qwk = [("W", c) for c in range(NCH)]
    owk = [("VW", c) for c in range(NCH)]
    for qb in range(NTB):
        if fused:
            @ngroup(NCH)
            def ldx(sq, qb=qb):
                return [sq.dma_start(out=b.XH[:, TB * ch:TB * (ch + 1)],
                                     in_=d["cinx"][P * ch:P * (ch + 1), TB * qb:TB * (qb + 1)])
                        for ch in range(NCH)]
            sch.op("sq", ldx, reads=[("cinx", qb)], writes=["XH"], dma=True)
        else:
            @ngroup(NCH)
            def ldx(sq, qb=qb):
                return [sq.dma_start(out=b.XSTG[:, TB * ch:TB * (ch + 1)],
                                     in_=d["xT"][P * ch:P * (ch + 1), TB * qb:TB * (qb + 1)])
                        for ch in range(NCH)]
            sch.op("sq", ldx, writes=["XSTG"], dma=True)
            emit_ln_tile(sch, b, b.XSTG, "XSTG", b.XH, "XH")
        for dch in range(NCH):
            def qm(pe, dch=dch):
                i = None
                for ch in range(NCH):
                    i = pe.matmul(b.psA[:, :], b.OW[:, DM * ch + P * dch: DM * ch + P * dch + P],
                                  b.XH[:, TB * ch:TB * (ch + 1)], start=(ch == 0), stop=(ch == NCH - 1))
                return i
            sch.op("pe", qm, reads=["XH"] + qwk, writes=["psA"])
            def qe(se, dch=dch):
                return se.activation(out=b.QB[:, TB * dch:TB * (dch + 1)], in_=b.psA[:, :],
                                     func=AF.Identity, scale=0.125, bias=b.BIA[:, dch:dch + 1])
            sch.op("se", qe, reads=["psA", "BIA"], writes=[("QB", dch)])
        qbk = [("QB", c) for c in range(NCH)]

        for hpi in range(8):
            h0, h1 = 2 * hpi, 2 * hpi + 1
            for kc in range(32):
                sp = kc % 2
                def ldvp(sq, qb=qb, hpi=hpi, kc=kc, sp=sp):
                    return sq.dma_start(out=b.VPS[:, 2080 * sp:2080 * sp + 1040],
                                        in_=d["vscr"][P * kc:P * (kc + 1), :])
                sch.op("sq", ldvp, reads=[("vscr", kc)], writes=[("VPS", sp)], dma=True)

                def sm(pe, h0=h0, h1=h1, kc=kc):
                    pe.matmul(b.pSa[:, :], _kt_head(b, h0, kc, NF), _qb_head(b, h0),
                              start=True, stop=True, tile_position=(0, 0))
                    return pe.matmul(b.pSb[:, :], _kt_head(b, h1, kc, NF), _qb_head(b, h1),
                                     start=True, stop=True, tile_position=(64, 0))
                sch.op("pe", sm, reads=[("KT", h0 // 2, kc // 4), ("QB", h0 // 2)],
                       writes=["pSa", "pSb"])

                def ex0(se, sp=sp):
                    return se.activation(out=b.PT[:, TB * (2 * sp):TB * (2 * sp + 1)], in_=b.pSa[:, :],
                                         func=AF.Exp, bias=cb(b, -10.0))
                sch.op("se", ex0, reads=["pSa", "CONSTS"], writes=[("PT", sp, 0)])
                def ex1(se, sp=sp):
                    return se.activation(out=b.PT[:, TB * (2 * sp + 1):TB * (2 * sp + 2)], in_=b.pSb[:, :],
                                         func=AF.Exp, bias=cb(b, -10.0))
                sch.op("se", ex1, reads=["pSb", "CONSTS"], writes=[("PT", sp, 1)])

                def pv(pe, h0=h0, h1=h1, kc=kc, sp=sp):
                    pe.matmul(b.pOa[0:65, :],
                              b.VPS[:, 2080 * sp + 65 * h0: 2080 * sp + 65 * h0 + 65],
                              b.PT[:, TB * (2 * sp):TB * (2 * sp + 1)],
                              start=(kc == 0), stop=(kc == 31))
                    return pe.matmul(b.pOb[0:65, :],
                                     b.VPS[:, 2080 * sp + 65 * h1: 2080 * sp + 65 * h1 + 65],
                                     b.PT[:, TB * (2 * sp + 1):TB * (2 * sp + 2)],
                                     start=(kc == 0), stop=(kc == 31))
                sch.op("pe", pv, reads=[("VPS", sp), ("PT", sp, 0), ("PT", sp, 1)],
                       writes=(["pOa", "pOb"] if kc in (0, 31) else []))

            for hh, pO in ((h0, b.pOa), (h1, b.pOb)):
                pok = "pOa" if hh == h0 else "pOb"
                def rec(ve, pO=pO):
                    return ve.reciprocal(b.ROWS[0:1, 0:TB], pO[64:65, :])
                sch.op("ve", rec, reads=[pok], writes=[("rowm",)])
                def rep(pe):
                    return pe.matmul(b.psY0[0:64, 0:TB], b.ONES[0:1, 0:64], b.ROWS[0:1, 0:TB],
                                     start=True, stop=True)
                sch.op("pe", rep, reads=[("rowm",), "ONES"], writes=["psY0"])
                def repe(se):
                    return se.activation(out=b.REP[0:64, :], in_=b.psY0[0:64, 0:TB], func=AF.Identity)
                sch.op("se", repe, reads=["psY0"], writes=["REP"])
                def nrm(ve, hh=hh, pO=pO):
                    return ve.tensor_mul(
                        b.CTX[64 * (hh % 2):64 * (hh % 2) + 64, TB * (hh // 2):TB * (hh // 2) + TB],
                        pO[0:64, :], b.REP[0:64, :])
                sch.op("ve", nrm, reads=[pok, "REP"], writes=[("CTX", hh)])

        ctxk = [("CTX", h) for h in range(16)]
        for dch in range(NCH):
            pp = dch % 2
            def om(pe, dch=dch):
                i = None
                for ch in range(NCH):
                    i = pe.matmul(b.psB[:, :], b.VW[:, DM * ch + P * dch: DM * ch + P * dch + P],
                                  b.CTX[:, TB * ch:TB * (ch + 1)], start=(ch == 0), stop=(ch == NCH - 1))
                return i
            sch.op("pe", om, reads=ctxk + owk, writes=["psB"])
            def ldxr(sq, qb=qb, dch=dch, pp=pp):
                return sq.dma_start(out=b.XR[:, TB * pp:TB * (pp + 1)],
                                    in_=d["xT"][P * dch:P * (dch + 1), TB * qb:TB * (qb + 1)])
            sch.op("sq", ldxr, writes=[("XR", pp)], dma=True)
            def res(ve, dch=dch, pp=pp):
                return ve.scalar_tensor_tensor(b.YS[:, TB * pp:TB * (pp + 1)], b.psB[:, :],
                                               b.BIA[:, 3 * NCH + dch:3 * NCH + dch + 1],
                                               b.XR[:, TB * pp:TB * (pp + 1)], ALU.add, ALU.add)
            sch.op("ve", res, reads=["psB", ("XR", pp), "BIA"], writes=[("YS", pp)])
            def sty(sq, qb=qb, dch=dch, pp=pp):
                return sq.dma_start(out=d["yT"][P * dch:P * (dch + 1), TB * qb:TB * (qb + 1)],
                                    in_=b.YS[:, TB * pp:TB * (pp + 1)])
            sch.op("sq", sty, reads=[("YS", pp)], writes=["OUT"], dma=True)

    def fin(sq):
        return sq.nop()
    sch.op("sq", fin, reads=["OUT"])


def _kt_head(b, h, kc, NF):
    dch, off = h // 2, 64 * (h % 2)
    return b.KT[off:off + 64, NF * dch + P * kc: NF * dch + P * (kc + 1)]


def _qb_head(b, h):
    dch, off = h // 2, 64 * (h % 2)
    return b.QB[off:off + 64, TB * dch:TB * (dch + 1)]


def host_inputs_l0attn(inp, core):
    s, half = core // 2, core % 2
    x = np.asarray(inp["L0_attn_query"])
    C = np.ascontiguousarray
    return {
        "xT": C(x[s, half * T:(half + 1) * T, :].T),
        "xTf": C(x[s].T),
        "qwT": C(np.asarray(inp["L0_attn_q_w"]).T),
        "kwT": C(np.asarray(inp["L0_attn_k_w"]).T),
        "vwT": C(np.asarray(inp["L0_attn_v_w"]).T),
        "owT": C(np.asarray(inp["L0_attn_out_w"]).T),
        "qb": np.asarray(inp["L0_attn_q_b"]),
        "kb": np.asarray(inp["L0_attn_k_b"]),
        "vb": np.asarray(inp["L0_attn_v_b"]),
        "ob": np.asarray(inp["L0_attn_out_b"]),
    }



def make_attn1_bufs(ctx, nc):
    sfx = "_u%d" % _BUFN[0]; _BUFN[0] += 1
    b = B()
    A = lambda name, shape, dt: ctx.enter_context(nc.sbuf_tensor(name + sfx, shape, dt))
    b.KT = A("KT", [P, NCH * 2 * T], BF16)
    b.QT = A("QT", [P, NCH * T], BF16)
    b.KO = A("KO", [P, NCH * TB], BF16)
    b.VO = A("VO", [P, 4 * DM], BF16)
    b.OW = A("OW", [P, NCH * DM], BF16)
    b.XSTG = A("XSTG", [P, NCH * TB], F32)
    b.XH = A("XH", [P, NCH * TB], BF16)
    b.XBT = A("XBT", [P, NCH * DM], BF16)
    b.SQT = b.XBT
    b.PT = A("PT", [P, 4 * TB], BF16)
    b.VPS = A("VPS", [P, 2 * DM], BF16)
    b.VST = A("VST", [P, DM], BF16)
    b.WSTG = A("WSTG", [P, DM], F32)
    b.ROWS = A("ROWS", [P, 3 * TB], F32)
    b.TMPT = A("TMPT", [P, 2 * TB], F32)
    b.REP = A("REP", [P, TB], F32)
    b.XR = A("XR", [P, TB], F32)
    b.YS = A("YS", [P, TB], F32)
    b.ACC8 = A("ACC8", [P, 16], F32)
    b.LBUF = A("LBUF", [P, 256], BF16)
    b.LROWS = A("LROWS", [P, T], BF16)
    b.EYE = A("EYE", [P, P], F32)
    b.TRI = A("TRI", [P, P], BF16)
    b.BCL = A("BCL", [P, P], F32)
    b.SEL = A("SEL", [P, NCH * P], BF16)
    b.ONES = A("ONES", [P, P], F32)
    b.ONESB = A("ONESB", [P, 8], BF16)
    b.CONSTS = A("CONSTS", [P, 8], F32)
    b.cidx = {}
    PA = lambda name, shape, dt=F32: ctx.enter_context(nc.psum_tensor(name + sfx, shape, dt))
    b.psA = PA("psA", [P, TB]); b.psB = PA("psB", [P, TB])
    b.psY0 = PA("psY0", [P, 2 * TB])
    b.pSa = PA("pSa", [P, TB]); b.pSb = PA("pSb", [P, TB])
    b.pO = PA("pO", [P, TB])
    b.psTB = PA("psTB", [P, 2 * TB], BF16)
    b.EYEB = A("EYEB", [P, P], BF16)

    return b


def build_l1attn_launch():
    nc = bass.Bass("TRN2", target_bir_lowering=False, debug=False)
    d = {}
    DI = lambda n, s: nc.dram_tensor(n, s, F32, kind="ExternalInput").ap()
    d["xT"] = DI("xT", [DM, T])
    d["xTf"] = DI("xTf", [DM, 2 * T])
    for w in ("wq", "wk", "wv"):
        d[w] = DI(w, [DM, DM])
    d["eye"] = DI("eye", [P, P])
    d["biascol"] = DI("biascol", [P, 128])
    d["sel"] = nc.dram_tensor("sel", [16, NCH * P], BF16, kind="ExternalInput").ap()
    d["tri"] = nc.dram_tensor("tri", [P, P], BF16, kind="ExternalInput").ap()
    d["yT"] = nc.dram_tensor("yT", [DM, T], F32, kind="ExternalOutput").ap()
    d["vscr"] = nc.dram_tensor("vscr", [32 * P, DM], BF16, kind="ExternalOutput").ap()
    with ExitStack() as ctx:
        sch = Sched(nc, ctx)
        b = make_attn1_bufs(ctx, nc)
        reg_consts(b, [1e-5, -10.0])
        emit_setup_consts_noeye(sch, b)
        emit_l1attn(sch, b, d)
        with nc.Block() as block:
            sch.build(block)
    return nc


def emit_l1attn(sch, b, d, fused=False):
    NF = 2 * T

    def ldc(sq):
        i1 = sq.dma_start(out=b.EYE[:, :], in_=d["eye"])
        i2 = sq.dma_start(out=b.TRI[:, :], in_=d["tri"])
        i3 = sq.dma_start(out=b.BCL[:, :], in_=d["biascol"])
        i4 = sq.dma_start(out=b.SEL[0:16, :], in_=d["sel"])
        return [i1, i2, i3, i4]
    ldc.n_insts = 4
    sch.op("sq", ldc, writes=["EYE", "TRI", "BCL", "SEL"], dma=True)
    def mkeyeb(ve):
        return ve.tensor_copy(b.EYEB[:, :], b.EYE[:, :])
    sch.op("ve", mkeyeb, reads=["EYE"], writes=["EYEB"])

    load_w_generic(sch, b, d, "wq", b.OW, "W", nping=1)
    qwk = [("W", c) for c in range(NCH)]
    for tb in range(NTB):
        @ngroup(NCH)
        def ldx(sq, tb=tb):
            return [sq.dma_start(out=b.XSTG[:, TB * ch:TB * (ch + 1)],
                                 in_=d["xT"][P * ch:P * (ch + 1), TB * tb:TB * (tb + 1)])
                    for ch in range(NCH)]
        sch.op("sq", ldx, writes=["XSTG"], dma=True)
        emit_ln_tile(sch, b, b.XSTG, "XSTG", b.XH, "XH")
        if fused:
            @ngroup(NCH)
            def stxh(sq, tb=tb):
                return [sq.dma_start(out=d["cinx"][P * ch:P * (ch + 1), TB * tb:TB * (tb + 1)],
                                     in_=b.XH[:, TB * ch:TB * (ch + 1)])
                        for ch in range(NCH)]
            sch.op("sq", stxh, reads=["XH"], writes=[("cinx", tb)], dma=True)
        for dch in range(NCH):
            def qm(pe, dch=dch):
                i = None
                for ch in range(NCH):
                    i = pe.matmul(b.psA[:, :], b.OW[:, DM * ch + P * dch: DM * ch + P * dch + P],
                                  b.XH[:, TB * ch:TB * (ch + 1)], start=(ch == 0), stop=(ch == NCH - 1))
                return i
            sch.op("pe", qm, reads=["XH"] + qwk, writes=["psA"])
            def qe(se, dch=dch, tb=tb):
                return se.activation(out=b.QT[:, T * dch + TB * tb: T * dch + TB * (tb + 1)],
                                     in_=b.psA[:, :], func=AF.Identity, scale=0.125)
            sch.op("se", qe, reads=["psA"], writes=[("QT", dch, tb)])
    qtk = [("QT", c, t_) for c in range(NCH) for t_ in range(NTB)]

    if fused:
        def ccx(gp):
            return gp.collective_compute("AllGather", ALU.bypass,
                                         replica_groups=[[0, 1], [2, 3], [4, 5], [6, 7]],
                                         ins=[d["cinx"].opt()], outs=[d["coutx"].opt()])
        sch.op("gp", ccx, reads=[("cinx", tb) for tb in range(NTB)], writes=["coutx"])

    load_w_generic(sch, b, d, "wk", b.OW, "W", nping=1)
    load_w_generic(sch, b, d, "wv", b.VWOW_dummy if False else b.OW, "W2") if False else None
    kwk = [("W", c) for c in range(NCH)]
    for tb in range(NF // TB):
        if fused:
            half, tloc = tb // 4, (tb % 4) * TB
            @ngroup(NCH)
            def ldxf(sq, half=half, tloc=tloc):
                return [sq.dma_start(out=b.XH[:, TB * ch:TB * (ch + 1)],
                                     in_=d["coutx"][DM * half + P * ch:DM * half + P * (ch + 1),
                                                    tloc:tloc + TB])
                        for ch in range(NCH)]
            sch.op("sq", ldxf, reads=["coutx"], writes=["XH"], dma=True)
        else:
            @ngroup(NCH)
            def ldxf(sq, tb=tb):
                return [sq.dma_start(out=b.XSTG[:, TB * ch:TB * (ch + 1)],
                                     in_=d["xTf"][P * ch:P * (ch + 1), TB * tb:TB * (tb + 1)])
                        for ch in range(NCH)]
            sch.op("sq", ldxf, writes=["XSTG"], dma=True)
            emit_ln_tile(sch, b, b.XSTG, "XSTG", b.XH, "XH")
        for dch in range(NCH):
            def km(pe, dch=dch):
                i = None
                for ch in range(NCH):
                    i = pe.matmul(b.psA[:, :], b.OW[:, DM * ch + P * dch: DM * ch + P * dch + P],
                                  b.XH[:, TB * ch:TB * (ch + 1)], start=(ch == 0), stop=(ch == NCH - 1))
                return i
            sch.op("pe", km, reads=["XH"] + kwk, writes=["psA"])
            def ke(se, dch=dch, tb=tb):
                return se.activation(out=b.KT[:, NF * dch + TB * tb: NF * dch + TB * (tb + 1)],
                                     in_=b.psA[:, :], func=AF.Identity)
            sch.op("se", ke, reads=["psA"], writes=[("KT", dch, tb)])

    load_w_generic(sch, b, d, "wv", b.OW, "W", nping=1)
    vwk = [("W", c) for c in range(NCH)]
    for tb in range(NF // TB):
        if fused:
            half, tloc = tb // 4, (tb % 4) * TB
            @ngroup(NCH)
            def ldxf2(sq, half=half, tloc=tloc):
                return [sq.dma_start(out=b.XH[:, TB * ch:TB * (ch + 1)],
                                     in_=d["coutx"][DM * half + P * ch:DM * half + P * (ch + 1),
                                                    tloc:tloc + TB])
                        for ch in range(NCH)]
            sch.op("sq", ldxf2, reads=["coutx"], writes=["XH"], dma=True)
        else:
            @ngroup(NCH)
            def ldxf2(sq, tb=tb):
                return [sq.dma_start(out=b.XSTG[:, TB * ch:TB * (ch + 1)],
                                     in_=d["xTf"][P * ch:P * (ch + 1), TB * tb:TB * (tb + 1)])
                        for ch in range(NCH)]
            sch.op("sq", ldxf2, writes=["XSTG"], dma=True)
            emit_ln_tile(sch, b, b.XSTG, "XSTG", b.XH, "XH")
        for j in range(4):
            tt = tb * 4 + j
            for bb in range(2):
                def vm(pe, j=j, bb=bb):
                    i = None
                    for ch in range(NCH):
                        i = pe.matmul(b.psB[:, :], b.XH[:, TB * ch + P * j: TB * ch + P * (j + 1)],
                                      b.OW[:, DM * ch + TB * bb: DM * ch + TB * (bb + 1)],
                                      start=(ch == 0), stop=(ch == NCH - 1))
                    return i
                sch.op("pe", vm, reads=["XH"] + vwk, writes=["psB"])
                def vev(ve, bb=bb):
                    return ve.tensor_copy(b.VST[:, TB * bb:TB * (bb + 1)], b.psB[:, :])
                sch.op("ve", vev, reads=["psB"], writes=[("VST", 0, bb)])
            def stv(sq, tt=tt):
                return sq.dma_start(out=d["vscr"][P * tt:P * (tt + 1), :], in_=b.VST[:, :])
            sch.op("sq", stv, reads=[("VST", 0, 0), ("VST", 0, 1)], writes=[("vscr", tt)], dma=True)

    for qt in range(NTT):
        for pr in range(8):
            h0, h1 = 2 * pr, 2 * pr + 1
            for kb in range(8):
                sp = kb % 2
                def sm(pe, qt=qt, pr=pr, kb=kb):
                    pe.matmul(b.pSa[:, :], _q_head(b, 2 * pr, qt), _kt_headr(b, 2 * pr, kb, NF),
                              start=True, stop=True, tile_position=(0, 0))
                    return pe.matmul(b.pSb[:, :], _q_head(b, 2 * pr + 1, qt), _kt_headr(b, 2 * pr + 1, kb, NF),
                                     start=True, stop=True, tile_position=(64, 0))
                sch.op("pe", sm, reads=[("QT", pr, qt // 4), ("KT", pr, kb)], writes=["pSa", "pSb"])
                def exa(se, qt=qt, pr=pr, kb=kb, sp=sp):
                    return se.activation(out=b.PT[:, TB * (2 * sp):TB * (2 * sp + 1)], in_=b.pSa[:, :],
                                         func=AF.Exp, bias=cb(b, -10.0), accum_out=b.ACC8[:, kb:kb + 1])
                sch.op("se", exa, reads=["pSa", "CONSTS"], writes=[("PTr", sp, 0), ("A8", kb)])
                def exb(se, qt=qt, pr=pr, kb=kb, sp=sp):
                    return se.activation(out=b.PT[:, TB * (2 * sp + 1):TB * (2 * sp + 2)], in_=b.pSb[:, :],
                                         func=AF.Exp, bias=cb(b, -10.0), accum_out=b.ACC8[:, 8 + kb:9 + kb])
                sch.op("se", exb, reads=["pSb", "CONSTS"], writes=[("PTr", sp, 1), ("A8", 8 + kb)])
            def lred(ve, qt=qt, pr=pr, nc=sch.nc):
                with nc.allow_low_precision("recip-l kept in bf16 deliberately"):
                    ve.reduce_sum(b.LBUF[:, 16 * qt + 2 * pr:16 * qt + 2 * pr + 1], b.ACC8[:, 0:8],
                                  axis=mybir.AxisListType.X)
                    return ve.reduce_sum(b.LBUF[:, 16 * qt + 2 * pr + 1:16 * qt + 2 * pr + 2],
                                         b.ACC8[:, 8:16], axis=mybir.AxisListType.X)
            sch.op("ve", lred, reads=[("A8", i) for i in range(16)],
                   writes=[("LB", qt, pr)])
    def lrec(ve, nc=None):
        with sch.nc.allow_low_precision("recip-l kept in bf16 deliberately"):
            return ve.reciprocal(b.LBUF[:, :], b.LBUF[:, :])
    sch.op("ve", lrec, reads=[("LB", qt, pr) for qt in range(NTT) for pr in range(8)],
           writes=["LBUFr"])
    for qt in range(NTT):
        def ltr(pe, qt=qt):
            return pe.transpose(b.psTB[0:16, 0:P], _lb_tile(b, qt), b.EYEB[:, :])
        sch.op("pe", ltr, reads=["LBUFr", "EYEB"], writes=["psTB"])
        def lte(ve, qt=qt):
            return ve.tensor_copy(b.LROWS[0:16, P * qt:P * (qt + 1)], b.psTB[0:16, 0:P])
        sch.op("ve", lte, reads=["psTB"], writes=[("LR", qt)])
    lrk = [("LR", qt) for qt in range(NTT)]

    for qb in range(NTB):
        if fused:
            @ngroup(NCH)
            def ldx2(sq, qb=qb):
                return [sq.dma_start(out=b.XH[:, TB * ch:TB * (ch + 1)],
                                     in_=d["cinx"][P * ch:P * (ch + 1), TB * qb:TB * (qb + 1)])
                        for ch in range(NCH)]
            sch.op("sq", ldx2, reads=[("cinx", qb)], writes=["XH"], dma=True)
        else:
            @ngroup(NCH)
            def ldx2(sq, qb=qb):
                return [sq.dma_start(out=b.XSTG[:, TB * ch:TB * (ch + 1)],
                                     in_=d["xT"][P * ch:P * (ch + 1), TB * qb:TB * (qb + 1)])
                        for ch in range(NCH)]
            sch.op("sq", ldx2, writes=["XSTG"], dma=True)
            emit_ln_tile(sch, b, b.XSTG, "XSTG", b.XH, "XH")
        load_w_generic(sch, b, d, "wk", b.OW, "W", nping=1)
        for dch in range(NCH):
            def kom(pe, dch=dch):
                i = None
                for ch in range(NCH):
                    i = pe.matmul(b.psA[:, :], b.OW[:, DM * ch + P * dch: DM * ch + P * dch + P],
                                  b.XH[:, TB * ch:TB * (ch + 1)], start=(ch == 0), stop=(ch == NCH - 1))
                return i
            sch.op("pe", kom, reads=["XH"] + [("W", c) for c in range(NCH)], writes=["psA"])
            def koe(se, dch=dch):
                return se.activation(out=b.KO[:, TB * dch:TB * (dch + 1)], in_=b.psA[:, :],
                                     func=AF.Identity)
            sch.op("se", koe, reads=["psA"], writes=[("KO", dch)])
        load_w_generic(sch, b, d, "wv", b.OW, "W", nping=1)
        for j in range(4):
            for bb in range(2):
                def vom(pe, j=j, bb=bb):
                    i = None
                    for ch in range(NCH):
                        i = pe.matmul(b.psB[:, :], b.XH[:, TB * ch + P * j: TB * ch + P * (j + 1)],
                                      b.OW[:, DM * ch + TB * bb: DM * ch + TB * (bb + 1)],
                                      start=(ch == 0), stop=(ch == NCH - 1))
                    return i
                sch.op("pe", vom, reads=["XH"] + [("W", c) for c in range(NCH)], writes=["psB"])
                def voe(ve, j=j, bb=bb):
                    return ve.tensor_copy(b.VO[:, DM * j + TB * bb: DM * j + TB * (bb + 1)], b.psB[:, :])
                sch.op("ve", voe, reads=["psB"], writes=[("VO", j, bb)])
        vok = [("VO", j, bb) for j in range(4) for bb in range(2)]
        kok = [("KO", c) for c in range(NCH)]

        for pr in range(8):
            h0, h1 = 2 * pr, 2 * pr + 1
            ndense = 16 + 4 * qb
            for kc in range(ndense):
                sp = kc % 2
                def ldvp(sq, kc=kc, sp=sp):
                    return sq.dma_start(out=b.VPS[:, DM * sp:DM * (sp + 1)],
                                        in_=d["vscr"][P * kc:P * (kc + 1), :])
                sch.op("sq", ldvp, reads=[("vscr", kc)], writes=[("VPS", sp)], dma=True)
                def sm2(pe, h0=h0, h1=h1, kc=kc, qb=qb):
                    pe.matmul(b.pSa[:, :], _kt_head(b, h0, kc, NF), _qt_head(b, h0, qb),
                              start=True, stop=True, tile_position=(0, 0))
                    return pe.matmul(b.pSb[:, :], _kt_head(b, h1, kc, NF), _qt_head(b, h1, qb),
                                     start=True, stop=True, tile_position=(64, 0))
                sch.op("pe", sm2, reads=[("KT", h0 // 2, kc // 4), ("QT", h0 // 2, qb)],
                       writes=["pSa", "pSb"])
                def ex0(se, sp=sp, kc=kc, qb=qb):
                    return se.activation(out=b.PT[:, TB * (2 * sp):TB * (2 * sp + 1)], in_=b.pSa[:, :],
                                         func=AF.Exp, bias=b.BCL[:, 4 * kc + qb:4 * kc + qb + 1])
                sch.op("se", ex0, reads=["pSa", "BCL"], writes=[("PT", sp, 0)])
                def ex1(se, sp=sp, kc=kc, qb=qb):
                    return se.activation(out=b.PT[:, TB * (2 * sp + 1):TB * (2 * sp + 2)], in_=b.pSb[:, :],
                                         func=AF.Exp, bias=b.BCL[:, 4 * kc + qb:4 * kc + qb + 1])
                sch.op("se", ex1, reads=["pSb", "BCL"], writes=[("PT", sp, 1)])
                def pv(pe, pr=pr, kc=kc, sp=sp, ndense=ndense):
                    pe.matmul(b.pO[0:64, :], b.VPS[:, DM * sp + P * pr: DM * sp + P * pr + 64],
                              b.PT[:, TB * (2 * sp):TB * (2 * sp + 1)],
                              start=(kc == 0), stop=False, tile_position=(0, 0))
                    return pe.matmul(b.pO[64:128, :], b.VPS[:, DM * sp + P * pr + 64: DM * sp + P * (pr + 1)],
                                     b.PT[:, TB * (2 * sp + 1):TB * (2 * sp + 2)],
                                     start=(kc == 0), stop=False, tile_position=(0, 64))
                sch.op("pe", pv, reads=[("VPS", sp), ("PT", sp, 0), ("PT", sp, 1)],
                       writes=(["pO"] if kc == 0 else []))
            for i in range(4):
                for j in range(i + 1):
                    spb = (i * 4 + j) % 2
                    def smb(pe, pr=pr, i=i, j=j, qb=qb):
                        pe.matmul(b.pSa[:, 0:P], _ko_head(b, 2 * pr, j), _qt_head_sub(b, 2 * pr, qb, i),
                                  start=True, stop=True, tile_position=(0, 0))
                        return pe.matmul(b.pSb[:, 0:P], _ko_head(b, 2 * pr + 1, j),
                                         _qt_head_sub(b, 2 * pr + 1, qb, i),
                                         start=True, stop=True, tile_position=(64, 0))
                    sch.op("pe", smb, reads=kok + [("QT", pr, qb)], writes=["pSa", "pSb"])
                    def exb0(se, spb=spb):
                        return se.activation(out=b.PT[:, TB * (2 * spb):TB * (2 * spb) + P], in_=b.pSa[:, 0:P],
                                             func=AF.Exp, bias=cb(b, -10.0))
                    sch.op("se", exb0, reads=["pSa", "CONSTS"], writes=[("PT", spb, 0)])
                    def exb1(se, spb=spb):
                        return se.activation(out=b.PT[:, TB * (2 * spb + 1):TB * (2 * spb + 1) + P],
                                             in_=b.pSb[:, 0:P], func=AF.Exp, bias=cb(b, -10.0))
                    sch.op("se", exb1, reads=["pSb", "CONSTS"], writes=[("PT", spb, 1)])
                    if j == i:
                        def msk0(ve, spb=spb):
                            return ve.tensor_mul(b.PT[:, TB * (2 * spb):TB * (2 * spb) + P],
                                                 b.PT[:, TB * (2 * spb):TB * (2 * spb) + P], b.TRI[:, :])
                        sch.op("ve", msk0, reads=[("PT", spb, 0), "TRI"], writes=[("PT", spb, 0)])
                        def msk1(ve, spb=spb):
                            return ve.tensor_mul(b.PT[:, TB * (2 * spb + 1):TB * (2 * spb + 1) + P],
                                                 b.PT[:, TB * (2 * spb + 1):TB * (2 * spb + 1) + P],
                                                 b.TRI[:, :])
                        sch.op("ve", msk1, reads=[("PT", spb, 1), "TRI"], writes=[("PT", spb, 1)])
                    last = (i == 3 and j == 3)
                    def pvb(pe, pr=pr, i=i, j=j, spb=spb, last=last):
                        pe.matmul(b.pO[0:64, P * i:P * (i + 1)],
                                  b.VO[:, DM * j + P * pr: DM * j + P * pr + 64],
                                  b.PT[:, TB * (2 * spb):TB * (2 * spb) + P],
                                  start=False, stop=last, tile_position=(0, 0))
                        return pe.matmul(b.pO[64:128, P * i:P * (i + 1)],
                                         b.VO[:, DM * j + P * pr + 64: DM * j + P * (pr + 1)],
                                         b.PT[:, TB * (2 * spb + 1):TB * (2 * spb + 1) + P],
                                         start=False, stop=last, tile_position=(0, 64))
                    sch.op("pe", pvb, reads=vok + [("PT", spb, 0), ("PT", spb, 1)],
                           writes=(["pO"] if last else []))

            def rep(pe, pr=pr, qb=qb):
                return pe.matmul(b.psY0[:, 0:TB], b.SEL[0:16, P * pr:P * (pr + 1)],
                                 b.LROWS[0:16, TB * qb:TB * (qb + 1)], start=True, stop=True)
            sch.op("pe", rep, reads=lrk + ["SEL"], writes=["psY0"])
            def repe(se):
                return se.activation(out=b.REP[:, :], in_=b.psY0[:, 0:TB], func=AF.Identity)
            sch.op("se", repe, reads=["psY0"], writes=["REP"])
            pp = pr % 2
            def nrm(ve, pp=pp):
                return ve.tensor_mul(b.TMPT[:, TB * pp:TB * (pp + 1)], b.pO[:, :], b.REP[:, :])
            sch.op("ve", nrm, reads=["pO", "REP"], writes=[("tmp", pp)])
            def ldxr(sq, qb=qb, pr=pr):
                return sq.dma_start(out=b.XR[:, :],
                                    in_=d["xT"][P * pr:P * (pr + 1), TB * qb:TB * (qb + 1)])
            sch.op("sq", ldxr, writes=[("XR", 0)], dma=True)
            def res(ve, pp=pp):
                return ve.tensor_add(b.YS[:, :], b.TMPT[:, TB * pp:TB * (pp + 1)], b.XR[:, :])
            sch.op("ve", res, reads=[("tmp", pp), ("XR", 0)], writes=[("YS", 0)])
            def sty(sq, qb=qb, pr=pr):
                return sq.dma_start(out=d["yT"][P * pr:P * (pr + 1), TB * qb:TB * (qb + 1)],
                                    in_=b.YS[:, :])
            sch.op("sq", sty, reads=[("YS", 0)], writes=["OUT"], dma=True)

    def fin(sq):
        return sq.nop()
    sch.op("sq", fin, reads=["OUT"])


def _q_head(b, h, qt):
    dch, off = h // 2, 64 * (h % 2)
    return b.QT[off:off + 64, T * dch + P * qt: T * dch + P * (qt + 1)]


def _qt_head(b, h, qb):
    dch, off = h // 2, 64 * (h % 2)
    return b.QT[off:off + 64, T * dch + TB * qb: T * dch + TB * (qb + 1)]


def _qt_head_sub(b, h, qb, i):
    dch, off = h // 2, 64 * (h % 2)
    return b.QT[off:off + 64, T * dch + TB * qb + P * i: T * dch + TB * qb + P * (i + 1)]


def _kt_headr(b, h, kb, NF):
    dch, off = h // 2, 64 * (h % 2)
    return b.KT[off:off + 64, NF * dch + TB * kb: NF * dch + TB * (kb + 1)]


def _ko_head(b, h, j):
    dch, off = h // 2, 64 * (h % 2)
    return b.KO[off:off + 64, TB * dch + P * j: TB * dch + P * (j + 1)]


def _lb_tile(b, qt):
    return b.LBUF[:, 16 * qt:16 * (qt + 1)]


def host_inputs_l1attn(inp, core, x_cur):
    s, half = core // 2, core % 2
    C = np.ascontiguousarray
    qoff = half * T
    bias = np.zeros((P, 128), np.float32)
    for kc in range(32):
        for qb in range(4):
            c0 = qoff // 128 + 4 * qb
            bias[:, 4 * kc + qb] = -10.0 if kc < c0 else -210.0
    import ml_dtypes
    tri = (np.arange(P)[:, None] <= np.arange(P)[None, :]).astype(ml_dtypes.bfloat16)
    import ml_dtypes
    sel = np.zeros((16, NCH * P), ml_dtypes.bfloat16)
    for pr in range(8):
        sel[2 * pr, P * pr:P * pr + 64] = 1.0
        sel[2 * pr + 1, P * pr + 64:P * (pr + 1)] = 1.0
    return {
        "sel": sel,
        "xT": C(x_cur[s, qoff:qoff + T, :].T),
        "xTf": C(x_cur[s].T),
        "wq": C(np.asarray(inp["L1_attn_Wq"])),
        "wk": C(np.asarray(inp["L1_attn_Wk"])),
        "wv": C(np.asarray(inp["L1_attn_Wv"])),
        "eye": np.eye(P, dtype=np.float32),
        "tri": tri,
        "biascol": bias,
    }



NHT = 6
NKT = 21
NQT = 17
HID = 2730
NHB = 22
HBW = [P] * 21 + [42]
RGRP = 3


def make_l1f_bufs(ctx, nc):
    sfx = "_u%d" % _BUFN[0]; _BUFN[0] += 1
    b = B()
    A = lambda name, shape, dt: ctx.enter_context(nc.sbuf_tensor(name + sfx, shape, dt))
    b.WQKV = A("WQKV", [P, NCH * 3 * TB], BF16)
    b.VL = A("VL", [P, NKT * TB], BF16)
    b.QTP = A("QTP", [P, 4 * NQT * P], BF16)
    b.KTP = A("KTP", [P, 4 * NKT * P], BF16)
    b.QKG = A("QKG", [P, RGRP * TB], F32)
    b.KGB = A("KGB", [P, RGRP * TB], BF16)
    b.QOUT = A("QOUT", [P, RGRP * TB], BF16)
    b.RT2 = A("RT2", [P, RGRP * TB], BF16)

    b.USTG = A("USTG", [P, NCH * (P + 1)], F32)
    b.H1T = A("H1T", [P, NCH * P], BF16)
    b.NRM = A("NRM", [P, RGRP * NCH * 2], F32)
    b.QSR = A("QSR", [P, 2 * TB], BF16)
    b.PL = A("PL", [P, 2 * 640], BF16)
    b.PTT = A("PTT", [P, 2 * 640], BF16)
    b.CTXT = A("CTXT", [P, 4 * TB], BF16)
    b.UOS = A("UOS", [P, NCH * TB], F32)
    b.UHLAST = A("UHLAST", [P, NCH * P], BF16)
    b.BC = A("BC", [P, NCH * 4], F32)
    b.XBT = A("XBT", [P, NCH * DM], BF16)
    b.SQT = b.XBT
    b.XSTG = A("XSTG", [P, NCH * TB], F32)
    b.XH = A("XH", [P, NCH * TB], BF16)
    b.WB1 = A("WB1", [P, NCH * P], BF16)
    b.WB2 = A("WB2", [P, P], BF16)
    b.BSGB = b.KGB
    b.WSTG = A("WSTG", [P, 2 * DM], F32)
    b.ROWS = A("ROWS", [P, 3 * TB], F32)
    b.TMPT = A("TMPT", [P, 2 * TB], F32)
    b.LNG = A("LNG", [P, NCH * 2], F32)
    b.LNBB = A("LNBB", [P, 2], BF16)
    b.LNBC = A("LNBC", [P, NCH], BF16)
    b.B1C = A("B1C", [P, 2 * NHB], F32)
    b.B2C = A("B2C", [P, NCH], F32)
    b.BSG = A("BSG", [P, P], F32)
    b.HB = A("HB", [P, 1], F32)
    b.HVC = A("HVC", [P, 1], F32)
    b.TRB = A("TRB", [P, 2 * P], BF16)
    b.EYEB = A("EYEB", [P, P], BF16)
    b.MX = A("MX", [P, 8], F32)
    b.LCOL = A("LCOL", [P, 4], F32)
    b.ONES = A("ONES", [P, P], F32)
    b.ONESB = A("ONESB", [P, 8], BF16)
    b.CONSTS = A("CONSTS", [P, 8], F32)
    b.cidx = {}
    PA = lambda name, shape, dt=F32: ctx.enter_context(nc.psum_tensor(name + sfx, shape, dt))
    b.psA = PA("psA", [P, TB]); b.psB = PA("psB", [P, TB])
    b.pSa = PA("pSa", [P, TB]); b.pSb = PA("pSb", [P, TB])
    b.pDa = PA("pDa", [P, TB]); b.pDb = PA("pDb", [P, TB])
    b.pO = PA("pO", [P, TB])
    b.psTB = PA("psTB", [P, 2 * TB], BF16)
    b.WO = b.WQKV
    b.TAB = b.UOS.ap()[:, 0:2 * RGRP * TB]
    b.INVE = b.XSTG.ap()[:, 0:RGRP * TB]
    b.QN = b.XSTG.ap()[:, RGRP * TB:2 * RGRP * TB]
    b.SQH = b.XSTG.ap()[:, RGRP * TB:2 * RGRP * TB]
    b.RT1 = b.ROWS.ap()[:, 0:RGRP * TB]

    return b


def build_l1f_layer_launch(first, last):
    nc = bass.Bass("TRN2", target_bir_lowering=False, debug=False)
    d = {}
    DI = lambda n, s, dt=F32: nc.dram_tensor(n, s, dt, kind="ExternalInput").ap()
    if first:
        d["xT"] = DI("xT", [DM, T])
        d["xhalT"] = DI("xhalT", [DM, NHT * P])
    else:
        d["uT"] = DI("uT", [DM, T])
        d["uhalT"] = DI("uhalT", [DM, NHT * P])
        if last:
            d["xT"] = DI("xT", [DM, T])
    d["wqkv"] = DI("wqkv", [DM, 3 * TB])
    d["wo"] = DI("wo", [TB, DM])
    d["w1"] = DI("w1", [DM, 2 * HID])
    d["b1"] = DI("b1", [2 * HID])
    d["w2"] = DI("w2", [HID, DM])
    d["b2"] = DI("b2", [DM])
    d["lng"] = DI("lng", [DM])
    d["lnb"] = DI("lnb", [DM])
    d["qs"] = DI("qs", [TB])
    d["ks"] = DI("ks", [TB])
    d["tabs"] = DI("tabs", [4, NKT * P, TB])
    d["hb"] = DI("hb", [P, 1])
    d["tribig"] = nc.dram_tensor("tribig", [P, 2 * P], BF16, kind="ExternalInput").ap()
    d["uscr"] = nc.dram_tensor("uscr", [DM, T + NHT * P], F32, kind="ExternalOutput").ap()
    d["u1scr"] = nc.dram_tensor("u1scr", [DM, T], F32, kind="ExternalOutput").ap()
    if last:
        d["yT"] = nc.dram_tensor("yT", [DM, T], F32, kind="ExternalOutput").ap()
    else:
        d["uoutT"] = nc.dram_tensor("uoutT", [DM, T], F32, kind="ExternalOutput").ap()
    with ExitStack() as ctx:
        sch = Sched(nc, ctx)
        b = make_l1f_bufs(ctx, nc)
        reg_consts(b, [1e-5, -10.0, 1e-30])
        emit_setup_consts_noeye(sch, b)
        emit_l1f_layer(sch, b, d, first, last)
        with nc.Block() as block:
            sch.build(block)
    return nc


def emit_l1f_layer(sch, b, d, first, last, fused=False):
    nc = sch.nc
    HTOK = NHT * P
    UW = T + HTOK

    @ngroup(3)
    def ldc(sq):
        i1 = sq.dma_start(out=b.HB[:, :], in_=d["hb"])
        i2 = sq.dma_start(out=b.TRB[:, :], in_=d["tribig"])
        i3 = sq.dma_start(out=b.ROWS[0:1, 0:TB], in_=d["qs"].rearrange("(o x) -> o x", o=1))
        return [i1, i2, i3]
    sch.op("sq", ldc, writes=["HB", "TRB", ("rowm",)], dma=True)
    def ldk(sq):
        return sq.dma_start(out=b.ROWS[0:1, TB:2 * TB], in_=d["ks"].rearrange("(o x) -> o x", o=1))
    sch.op("sq", ldk, writes=[("rowA",)], dma=True)
    if fused:
        def ldhv(sq):
            return sq.dma_start(out=b.HVC[:, :], in_=d["hv"])
        sch.op("sq", ldhv, writes=["HVC"], dma=True)
    def qsrep(pe):
        pe.matmul(b.psA[:, :], b.ONES[0:1, 0:P], b.ROWS[0:1, 0:TB], start=True, stop=True)
        return pe.matmul(b.psB[:, :], b.ONES[0:1, 0:P], b.ROWS[0:1, TB:2 * TB], start=True, stop=True)
    sch.op("pe", qsrep, reads=[("rowm",), ("rowA",), "ONES"], writes=["psA", "psB"])
    def qsrepe(ve):
        ve.tensor_copy(b.QSR[:, 0:TB], b.psA[:, :])
        return ve.tensor_copy(b.QSR[:, TB:2 * TB], b.psB[:, :])
    sch.op("ve", qsrepe, reads=["psA", "psB"], writes=["QSR"])
    @ngroup(3)
    def ldb2(sq):
        i1 = sq.dma_start(out=b.LNG[:, 0:NCH], in_=d["lng"].rearrange("(c p) -> p c", p=P),
                          allow_slow_non_contiguous=True)
        i2 = sq.dma_start(out=b.LNG[:, NCH:2 * NCH], in_=d["lnb"].rearrange("(c p) -> p c", p=P),
                          allow_slow_non_contiguous=True)
        i3 = sq.dma_start(out=b.B2C[:, :], in_=d["b2"].rearrange("(c p) -> p c", p=P),
                          allow_slow_non_contiguous=True)
        return [i1, i2, i3]
    sch.op("sq", ldb2, writes=["LNG", "B2C"], dma=True)
    def lnbc(ve):
        return ve.tensor_copy(b.LNBC[:, :], b.LNG[:, NCH:2 * NCH])
    sch.op("ve", lnbc, reads=["LNG"], writes=["LNBC"])

    def mkeye(ve):
        return ve.tensor_add(b.BSG[:, :], b.TRB[:, 0:P], b.TRB[:, P:2 * P])
    sch.op("ve", mkeye, reads=["TRB"], writes=["BSG0"])
    def mkeye2(se):
        return se.activation(out=b.EYEB[:, :], in_=b.BSG[:, :], func=AF.Exp)
    sch.op("se", mkeye2, reads=["BSG0"], writes=["EYEB"])

    for ch in range(NCH):
        def ldw(sq, ch=ch):
            return sq.dma_start(out=b.WSTG[:, 0:DM], in_=d["wqkv"][P * ch:P * (ch + 1), 0:DM])
        sch.op("sq", ldw, writes=[("WSTG", 0)], dma=True)
        def cw(gp, ch=ch):
            return gp.tensor_copy(b.WQKV[:, 1536 * ch: 1536 * ch + DM], b.WSTG[:, 0:DM])
        sch.op("gp", cw, reads=[("WSTG", 0)], writes=[("WQKV", ch, 0)])
        def ldwq2(sq, ch=ch):
            return sq.dma_start(out=b.WSTG[:, 0:TB], in_=d["wqkv"][P * ch:P * (ch + 1), DM:3 * TB])
        sch.op("sq", ldwq2, writes=[("WSTG", 0)], dma=True)
        def cw2(gp, ch=ch):
            return gp.tensor_copy(b.WQKV[:, 1536 * ch + DM: 1536 * (ch + 1)], b.WSTG[:, 0:TB])
        sch.op("gp", cw2, reads=[("WSTG", 0)], writes=[("WQKV", ch, 1)])
    wqkvk = [("WQKV", c, i) for c in range(NCH) for i in range(2)]


    if first:
        starts = ([HTOK + TB * i for i in range(NTB)] if fused else
                  [0, HTOK - TB] + [HTOK + TB * i for i in range(NTB)])
        for tb, c0 in enumerate(starts):
            pieces = []
            if c0 < HTOK:
                w1_ = min(TB, HTOK - c0)
                pieces.append(("xhalT", c0, 0, w1_))
                if w1_ < TB:
                    pieces.append(("xT", 0, w1_, TB - w1_))
            else:
                pieces.append(("xT", c0 - HTOK, 0, TB))
            @ngroup(NCH * len(pieces))
            def ldxf(sq, tb=tb, pieces=tuple(pieces)):
                res = []
                for ch in range(NCH):
                    for (nm, sc, do, w) in pieces:
                        res.append(sq.dma_start(
                            out=b.XSTG[:, TB * ch + do:TB * ch + do + w],
                            in_=d[nm][P * ch:P * (ch + 1), sc:sc + w]))
                return res
            sch.op("sq", ldxf, writes=["XSTG"], dma=True)
            emit_ln_tile(sch, b, b.XSTG, "XSTG", b.UOS, "UOSln")
            @ngroup(NCH)
            def stu(sq, c0=c0):
                return [sq.dma_start(out=d["uscr"][P * ch:P * (ch + 1), c0:c0 + TB],
                                     in_=b.UOS[:, TB * ch:TB * (ch + 1)])
                        for ch in range(NCH)]
            sch.op("sq", stu, reads=["UOSln"], writes=[("uscr", tb)], dma=True)
        usck = [("uscr", t_) for t_ in range(len(starts))]
    else:
        usrc = None
        usck = []

    if fused:
        for r in range(2):
            @ngroup(NCH)
            def gld(sq, r=r):
                return [sq.dma_start(out=b.XSTG[:, 384 * ch:384 * (ch + 1)],
                                     in_=d["uscr"][P * ch:P * (ch + 1),
                                                   UW - HTOK + 384 * r:UW - HTOK + 384 * (r + 1)])
                        for ch in range(NCH)]
            sch.op("sq", gld, reads=list(usck) + ["uown"], writes=["XSTG"], dma=True)
            @ngroup(NCH)
            def gst(sq, r=r):
                return [sq.dma_start(out=d["cinu"][P * ch:P * (ch + 1), 384 * r:384 * (r + 1)],
                                     in_=b.XSTG[:, 384 * ch:384 * (ch + 1)])
                        for ch in range(NCH)]
            sch.op("sq", gst, reads=["XSTG"], writes=[("cinu", r)], dma=True)
        def ccu(gp):
            return gp.collective_compute("AllGather", ALU.bypass,
                                         replica_groups=[[0, 1], [2, 3], [4, 5], [6, 7]],
                                         ins=[d["cinu"].opt()], outs=[d["coutu"].opt()])
        sch.op("gp", ccu, reads=[("cinu", r) for r in range(2)], writes=["coutu"])
        for r in range(2):
            @ngroup(NCH)
            def hld(sq, r=r):
                return [sq.dma_start(out=b.XSTG[:, 384 * ch:384 * (ch + 1)],
                                     in_=d["coutu"][P * ch:P * (ch + 1), 384 * r:384 * (r + 1)])
                        for ch in range(NCH)]
            sch.op("sq", hld, reads=["coutu"], writes=["XSTG"], dma=True)
            @ngroup(NCH)
            def hst(sq, r=r):
                return [sq.dma_start(out=d["uscr"][P * ch:P * (ch + 1), 384 * r:384 * (r + 1)],
                                     in_=b.XSTG[:, 384 * ch:384 * (ch + 1)])
                        for ch in range(NCH)]
            sch.op("sq", hst, reads=["XSTG"], writes=[("uscrH", r)] + list(usck), dma=True)
        usck = usck + [("uscrH", 0), ("uscrH", 1)]

    def u_ap(ch, c0, w):
        if first or fused:
            return d["uscr"][P * ch:P * (ch + 1), c0:c0 + w]
        if c0 < HTOK:
            assert c0 + w <= HTOK
            return d["uhalT"][P * ch:P * (ch + 1), c0:c0 + w]
        return d["uT"][P * ch:P * (ch + 1), c0 - HTOK:c0 - HTOK + w]

    for g in range(7):
        for ti in range(RGRP):
            tl = 3 * g + ti
            ucol = P * (tl + 1)
            @ngroup(NCH + 1)
            def ldu(sq, ucol=ucol):
                res = [sq.dma_start(out=b.USTG[:, (P + 1) * ch + 1:(P + 1) * (ch + 1)],
                                    in_=u_ap(ch, ucol, P)) for ch in range(NCH)]
                res.append(sq.dma_start(out=b.USTG.ap().rearrange("p (c e) -> p c e", c=NCH)[:, :, 0:1],
                                        in_=_ucols(d, first, ucol - 1),
                                        allow_slow_non_contiguous=True))
                return res
            sch.op("sq", ldu, reads=list(usck), writes=["USTG"], dma=True)
            if fused and tl <= 5:
                def hvm(ve, tl=tl):
                    if tl <= 4:
                        return ve.tensor_scalar_mul(b.USTG[:, :], b.USTG[:, :], b.HVC[:, 0:1])
                    return ve.tensor_scalar_mul(
                        b.USTG.ap().rearrange("p (c e) -> p c e", c=NCH)[:, :, 0:1],
                        b.USTG.ap().rearrange("p (c e) -> p c e", c=NCH)[:, :, 0:1],
                        b.HVC[:, 0:1])
                sch.op("ve", hvm, reads=["USTG", "HVC"], writes=["USTG"])
            def h1c(gp):
                out = []
                for ch in range(NCH):
                    src = b.USTG[:, (P + 1) * ch + 1:(P + 1) * (ch + 1)] if ch < 4 else \
                          b.USTG[:, (P + 1) * ch:(P + 1) * (ch + 1) - 1]
                    out.append(gp.tensor_copy(b.H1T[:, P * ch:P * (ch + 1)], src))
                return out[-1]
            sch.op("gp", h1c, reads=["USTG"], writes=["H1T"])
            for blk in range(3):
                def qkvm(pe, blk=blk):
                    i = None
                    for ch in range(NCH):
                        i = pe.matmul(b.psA[:, :] if blk < 2 else b.psB[:, :],
                                      b.H1T[:, P * ch:P * (ch + 1)],
                                      b.WQKV[:, 1536 * ch + TB * blk: 1536 * ch + TB * (blk + 1)],
                                      start=(ch == 0), stop=(ch == NCH - 1))
                    return i
                psk = "psA" if blk < 2 else "psB"
                sch.op("pe", qkvm, reads=["H1T"] + wqkvk, writes=[psk])
                if blk < 2:
                    def qkve(se, blk=blk, ti=ti):
                        dstb = b.QKG if blk == 0 else b.KGB
                        return se.activation(out=dstb[:, TB * ti:TB * (ti + 1)],
                                             in_=b.psA[:, :], func=AF.Identity)
                    sch.op("se", qkve, reads=["psA"], writes=[("QKG", blk, ti)])
                else:
                    def ve2(ve, tl=tl):
                        return ve.tensor_copy(b.VL[:, TB * tl:TB * (tl + 1)], b.psB[:, :])
                    sch.op("ve", ve2, reads=["psB"], writes=[("VL", tl)])
        for blk in range(2):
            @ngroup(2)
            def ldtab(sq, g=g, blk=blk):
                return [sq.dma_start(out=b.TAB[:, RGRP * TB * i:RGRP * TB * (i + 1)]
                                     .rearrange("p (t e) -> p t e", t=RGRP),
                                     in_=d["tabs"][2 * blk + i, RGRP * P * g:RGRP * P * (g + 1), :]
                                     .rearrange("(t p) e -> p t e", p=P))
                        for i in range(2)]
            sch.op("sq", ldtab, writes=["TAB"] + [("UOS", c) for c in range(NCH)] + ["UOSln"],
                   dma=True)
            gk = [("QKG", blk, ti) for ti in range(RGRP)]
            Qg = (b.QKG if blk == 0 else b.KGB)[:, :]
            def sqr(se, Qg=Qg):
                return se.activation(out=b.SQH[:, :], in_=Qg, func=AF.Square)
            sch.op("se", sqr, reads=gk, writes=["SQH", "QN", "XSTG"])
            def nsum(ve, blk=blk):
                return ve.reduce_sum(
                    b.NRM.ap().rearrange("p (bb t h) -> p bb t h", bb=2, t=RGRP)[:, blk],
                    b.SQH.rearrange("p (t h e) -> p t h e", t=RGRP, h=NCH),
                    axis=mybir.AxisListType.X)
            sch.op("ve", nsum, reads=["SQH"], writes=[("NRM", blk)])
            def nsqrt(se, blk=blk):
                return se.activation(out=b.NRM[:, 24 * blk:24 * (blk + 1)],
                                     in_=b.NRM[:, 24 * blk:24 * (blk + 1)], func=AF.Sqrt)
            sch.op("se", nsqrt, reads=[("NRM", blk)], writes=[("NRM", blk)])
            def nclip(ve, blk=blk):
                return ve.tensor_scalar_max(b.NRM[:, 24 * blk:24 * (blk + 1)],
                                            b.NRM[:, 24 * blk:24 * (blk + 1)], 1e-12)
            sch.op("ve", nclip, reads=[("NRM", blk)], writes=[("NRM", blk)])
            def nrec(ve, blk=blk):
                return ve.reciprocal(b.NRM[:, 24 * blk:24 * (blk + 1)],
                                     b.NRM[:, 24 * blk:24 * (blk + 1)])
            sch.op("ve", nrec, reads=[("NRM", blk)], writes=[("NRM", blk)])
            def nexp(ve, blk=blk):
                return ve.tensor_copy(
                    b.INVE.rearrange("p (t h e) -> p t h e", t=RGRP, h=NCH),
                    b.NRM.ap().rearrange("p (b t h e) -> p b t h e", b=2, t=RGRP, h=NCH, e=1)
                    [:, blk].to_broadcast([P, RGRP, NCH, 64]))
            sch.op("ve", nexp, reads=[("NRM", blk)], writes=["INVE", "XSTG"])
            def nqs(ve, blk=blk):
                return ve.scalar_tensor_tensor(
                    b.INVE.rearrange("p (g x) -> p g x", g=1)[:, 0],
                    b.INVE.rearrange("p (g x) -> p g x", g=1)[:, 0],
                    8.0 if blk == 0 else 1.0,
                    b.QSR[:, TB * blk:TB * (blk + 1)].rearrange("p (o e) -> p o e", o=1)
                    .to_broadcast([P, RGRP, TB]),
                    ALU.mult, ALU.mult)
            sch.op("ve", nqs, reads=["INVE", "QSR"], writes=["INVE"])
            toff = 0
            def r1(ve, Qg=Qg, blk=blk):
                return ve.tensor_mul(b.QN[:, :], Qg, b.INVE[:, :])
            sch.op("ve", r1, reads=gk + ["INVE"], writes=["QN", "SQH"])
            def r2(ve, blk=blk, toff=toff):
                return ve.tensor_mul(b.RT1[:, :], b.QN[:, :], b.TAB[:, toff:toff + RGRP * TB])
            sch.op("ve", r2, reads=["QN", "TAB"],
                   writes=["RT1", ("rowm",), ("rowA",), ("rowB",), ("rowt",)])
            def r3a(ve, blk=blk, toff=toff):
                v = b.QN.rearrange("p (t h e) -> p t h e", t=RGRP, h=NCH)
                s = b.TAB.rearrange("p (f t h e) -> p f t h e", f=2, t=RGRP, h=NCH)
                return ve.tensor_mul(
                    b.RT2.ap().rearrange("p (t h e) -> p t h e", t=RGRP, h=NCH)[:, :, :, 0:32],
                    v[:, :, :, 32:64],
                    s[:, 1, :, :, 0:32])
            sch.op("ve", r3a, reads=["QN", "TAB"], writes=["RT2a"])
            def r3b(ve, blk=blk):
                v = b.QN.rearrange("p (t h e) -> p t h e", t=RGRP, h=NCH)
                s = b.TAB.rearrange("p (f t h e) -> p f t h e", f=2, t=RGRP, h=NCH)
                return ve.tensor_mul(
                    b.RT2.ap().rearrange("p (t h e) -> p t h e", t=RGRP, h=NCH)[:, :, :, 32:64],
                    v[:, :, :, 0:32],
                    s[:, 1, :, :, 32:64])
            sch.op("ve", r3b, reads=["QN", "TAB"], writes=["RT2b"])
            def r4(ve, blk=blk):
                return ve.tensor_add(b.QOUT[:, :], b.RT1[:, :], b.RT2[:, :])
            sch.op("ve", r4, reads=["RT1", "RT2a", "RT2b", ("rowm",)], writes=["QOUT"])
            dst = b.QTP if blk == 0 else b.KTP
            dk = "QTP" if blk == 0 else "KTP"
            for ti in range(RGRP):
                tl = 3 * g + ti
                if blk == 0 and tl < 4:
                    continue
                def ttr(pe, ti=ti):
                    i = None
                    for c4 in range(4):
                        i = pe.transpose(b.psTB[:, P * c4:P * (c4 + 1)],
                                         b.QOUT[:, TB * ti + P * c4:TB * ti + P * (c4 + 1)],
                                         b.EYEB[:, :])
                    return i
                sch.op("pe", ttr, reads=["QOUT", ("rowm",)], writes=["psTB"])
                dtl = tl - 4 if blk == 0 else tl
                def tte(ve, dtl=dtl, dst=dst, dk=dk, blk=blk):
                    return ve.tensor_copy(
                        dst.ap().rearrange("p (c t) -> p c t", c=4)[:, :, P * dtl:P * (dtl + 1)],
                        b.psTB.ap().rearrange("p (c t) -> p c t", c=8)[:, 0:4, :])
                sch.op("ve", tte, reads=["psTB"], writes=[(dk, dtl)])

    for ch in range(4):
        def ldwo(sq, ch=ch):
            return sq.dma_start(out=b.WSTG[:, 0:DM], in_=d["wo"][P * ch:P * (ch + 1), :])
        sch.op("sq", ldwo, writes=[("WSTG", 0)], dma=True)
        def cwo(gp, ch=ch):
            return gp.tensor_copy(b.WO[:, DM * ch:DM * (ch + 1)], b.WSTG[:, 0:DM])
        wr3 = [("WO", ch)] + (wqkvk if ch == 0 else [])
        sch.op("gp", cwo, reads=[("WSTG", 0)], writes=wr3)
    wok = [("WO", c) for c in range(4)]
    qtpk = [("QTP", t_) for t_ in range(NQT)]
    ktpk = [("KTP", t_) for t_ in range(NKT)]
    for iq in range(NQT):
        qtl = iq + 4
        blkq = (iq - 1) // 4
        for pr in range(4):
            h0, h1 = 2 * pr, 2 * pr + 1
            def sm(pe, qtl=qtl, h0=h0, h1=h1):
                pe.matmul(b.pSa[:, :], _qtp_head(b, h0, qtl), _ktp_range(b, h0, qtl - 4, 4),
                          start=True, stop=True, tile_position=(0, 0))
                pe.matmul(b.pSb[:, :], _qtp_head(b, h1, qtl), _ktp_range(b, h1, qtl - 4, 4),
                          start=True, stop=True, tile_position=(64, 0))
                pe.matmul(b.pDa[:, 0:P], _qtp_head(b, h0, qtl), _ktp_range(b, h0, qtl, 1),
                          start=True, stop=True, tile_position=(0, 0))
                return pe.matmul(b.pDb[:, 0:P], _qtp_head(b, h1, qtl), _ktp_range(b, h1, qtl, 1),
                                 start=True, stop=True, tile_position=(64, 0))
            sch.op("pe", sm, reads=qtpk + ktpk, writes=["pSa", "pSb", "pDa", "pDb"])
            nhalo = max(0, min(4, 4 - iq + 1 - 1))
            nhalo = max(0, min(4, 5 - iq))
            for ps, psk in ((b.pSa, "pSa"), (b.pSb, "pSb")):
                if nhalo > 0:
                    def hbadd(ve, ps=ps, nhalo=nhalo):
                        return ve.tensor_scalar_add(ps[:, 0:P * nhalo], ps[:, 0:P * nhalo],
                                                    b.HB[:, 0:1])
                    sch.op("ve", hbadd, reads=[psk, "HB"], writes=[psk])
                def far(ve, ps=ps):
                    return ve.tensor_add(ps[:, 0:P], ps[:, 0:P], b.TRB[:, P:2 * P])
                sch.op("ve", far, reads=[psk, "TRB"], writes=[psk])
            for ps, psk in ((b.pDa, "pDa"), (b.pDb, "pDb")):
                if iq == 0:
                    def hbd(ve, ps=ps):
                        return ve.tensor_scalar_add(ps[:, 0:P], ps[:, 0:P], b.HB[:, 0:1])
                    sch.op("ve", hbd, reads=[psk, "HB"], writes=[psk])
                def dg(ve, ps=ps):
                    return ve.tensor_add(ps[:, 0:P], ps[:, 0:P], b.TRB[:, 0:P])
                sch.op("ve", dg, reads=[psk, "TRB"], writes=[psk])
            for hh, pS, pD, psk, pdk in ((0, b.pSa, b.pDa, "pSa", "pDa"),
                                         (1, b.pSb, b.pDb, "pSb", "pDb")):
                def mx1(ve, pS=pS, hh=hh):
                    return ve.reduce_max(b.MX[:, 2 * hh:2 * hh + 1], pS[:, :],
                                         axis=mybir.AxisListType.X)
                sch.op("ve", mx1, reads=[psk], writes=[("MX", hh, 0)])
                def mx2(ve, pD=pD, hh=hh):
                    return ve.reduce_max(b.MX[:, 2 * hh + 1:2 * hh + 2], pD[:, 0:P],
                                         axis=mybir.AxisListType.X)
                sch.op("ve", mx2, reads=[pdk], writes=[("MX", hh, 1)])
                def mx3(ve, hh=hh):
                    return ve.tensor_max(b.MX[:, 4 + hh:5 + hh], b.MX[:, 2 * hh:2 * hh + 1],
                                         b.MX[:, 2 * hh + 1:2 * hh + 2])
                sch.op("ve", mx3, reads=[("MX", hh, 0), ("MX", hh, 1)], writes=[("MXc", hh)])
                def mx4(ve, hh=hh):
                    return ve.tensor_scalar_mul(b.MX[:, 6 + hh:7 + hh], b.MX[:, 4 + hh:5 + hh], -1.0)
                sch.op("ve", mx4, reads=[("MXc", hh)], writes=[("MXn", hh)])
                def exw(se, pS=pS, hh=hh):
                    return se.activation(out=b.PL[:, 640 * hh:640 * hh + TB], in_=pS[:, :],
                                         func=AF.Exp, bias=b.MX[:, 6 + hh:7 + hh],
                                         accum_out=b.LCOL[:, 2 * hh:2 * hh + 1])
                sch.op("se", exw, reads=[psk, ("MXn", hh)], writes=[("PL", hh, 0), ("LC", hh, 0)])
                def exd(se, pD=pD, hh=hh):
                    return se.activation(out=b.PL[:, 640 * hh + TB:640 * (hh + 1)], in_=pD[:, 0:P],
                                         func=AF.Exp, bias=b.MX[:, 6 + hh:7 + hh],
                                         accum_out=b.LCOL[:, 2 * hh + 1:2 * hh + 2])
                sch.op("se", exd, reads=[pdk, ("MXn", hh)], writes=[("PL", hh, 1), ("LC", hh, 1)])
                def lfin(ve, hh=hh):
                    return ve.tensor_add(b.LCOL[:, 2 * hh:2 * hh + 1], b.LCOL[:, 2 * hh:2 * hh + 1],
                                         b.LCOL[:, 2 * hh + 1:2 * hh + 2])
                sch.op("ve", lfin, reads=[("LC", hh, 0), ("LC", hh, 1)], writes=[("LC2", hh)])
                def lfin2(ve, hh=hh):
                    return ve.tensor_scalar_max(b.LCOL[:, 2 * hh:2 * hh + 1],
                                                b.LCOL[:, 2 * hh:2 * hh + 1], 1e-30)
                sch.op("ve", lfin2, reads=[("LC2", hh)], writes=[("LC3", hh)])
                def lfin3(ve, hh=hh):
                    return ve.reciprocal(b.LCOL[:, 2 * hh:2 * hh + 1], b.LCOL[:, 2 * hh:2 * hh + 1])
                sch.op("ve", lfin3, reads=[("LC3", hh)], writes=[("LR", hh)])
                def pnorm(ve, hh=hh):
                    return ve.tensor_scalar_mul(b.PL[:, 640 * hh:640 * (hh + 1)],
                                                b.PL[:, 640 * hh:640 * (hh + 1)],
                                                b.LCOL[:, 2 * hh:2 * hh + 1])
                sch.op("ve", pnorm, reads=[("PL", hh, 0), ("PL", hh, 1), ("LR", hh)],
                       writes=[("PL", hh, 0), ("PL", hh, 1)])
                def ptr(pe, hh=hh):
                    i = None
                    for c5 in range(5):
                        i = pe.transpose(b.psTB[:, P * c5:P * (c5 + 1)],
                                         b.PL[:, 640 * hh + P * c5:640 * hh + P * (c5 + 1)],
                                         b.EYEB[:, :])
                    return i
                sch.op("pe", ptr, reads=[("PL", hh, 0), ("PL", hh, 1), "EYEB"], writes=["psTB"])
                def pte(ve, hh=hh):
                    return ve.tensor_copy(b.PTT[:, 640 * hh:640 * (hh + 1)], b.psTB[:, 0:640])
                sch.op("ve", pte, reads=["psTB"], writes=[("PTT", hh)])
            def pv(pe, qtl=qtl, pr=pr):
                i = None
                for c5 in range(5):
                    tl = qtl - 4 + c5
                    for hh in range(2):
                        i = pe.matmul(b.pO[64 * hh:64 * (hh + 1), P * pr:P * (pr + 1)],
                                      b.VL[:, TB * tl + P * pr + 64 * hh:TB * tl + P * pr + 64 * (hh + 1)],
                                      b.PTT[:, 640 * hh + P * c5:640 * hh + P * (c5 + 1)],
                                      start=(c5 == 0), stop=(c5 == 4),
                                      tile_position=(0, 64 * hh))
                return i
            sch.op("pe", pv, reads=[("PTT", 0), ("PTT", 1)] + [("VL", qtl - 4 + c) for c in range(5)],
                   writes=[("pO", pr)])
        bi = (iq - 1) % 4 if iq >= 1 else 0
        def ctxe(se, iq=iq, bi=bi):
            return se.activation(out=_ctxt_cols(b, bi),
                                 in_=b.pO.ap().rearrange("p (c t) -> p c t", c=4),
                                 func=AF.Identity)
        sch.op("se", ctxe, reads=[("pO", p_) for p_ in range(4)], writes=[("CTXT", bi)])
        if iq == 0:
            @ngroup(NCH)
            def ldum1(sq):
                return [sq.dma_start(out=b.XSTG[:, TB * ch:TB * ch + P],
                                     in_=u_ap(ch, NHT * P - P, P)) for ch in range(NCH)]
            sch.op("sq", ldum1, reads=list(usck),
                   writes=["XSTG", ("XSTGm1",), "INVE", "QN", "SQH"], dma=True)
            if fused:
                def hvm1(ve):
                    v = b.XSTG.ap().rearrange("p (c e) -> p c e", c=NCH)[:, :, 0:P]
                    return ve.tensor_scalar_mul(v, v, b.HVC[:, 0:1])
                sch.op("ve", hvm1, reads=[("XSTGm1",), "XSTG", "HVC"], writes=[("XSTGm1",), "XSTG"])
            for dch in range(NCH):
                def wom(pe, dch=dch):
                    i = None
                    for c4 in range(4):
                        i = pe.matmul(b.psA[:, 0:P], b.WO[:, DM * c4 + P * dch: DM * c4 + P * (dch + 1)],
                                      _ctxt_chunk(b, c4)[:, 0:P], start=(c4 == 0), stop=(c4 == 3))
                    return i
                sch.op("pe", wom, reads=[("CTXT", 0)] + wok, writes=["psA"])
                def ures(ve, dch=dch):
                    return ve.tensor_add(b.UHLAST[:, P * dch:P * (dch + 1)], b.psA[:, 0:P],
                                         b.XSTG[:, TB * dch:TB * dch + P])
                sch.op("ve", ures, reads=["psA", ("XSTGm1",), "XSTG"], writes=[("UHLAST", dch)])
        elif bi == 3:
            blk = (iq - 1) // 4
            for dch in range(NCH):
                def wom(pe, dch=dch):
                    i = None
                    for c4 in range(4):
                        i = pe.matmul(b.psA[:, :], b.WO[:, DM * c4 + P * dch: DM * c4 + P * (dch + 1)],
                                      _ctxt_chunk(b, c4), start=(c4 == 0), stop=(c4 == 3))
                    return i
                sch.op("pe", wom, reads=[("CTXT", j) for j in range(4)] + wok, writes=["psA"])
                def ldur(sq, blk=blk, dch=dch):
                    return sq.dma_start(out=b.XSTG[:, TB * dch:TB * (dch + 1)],
                                        in_=u_ap(dch, NHT * P + TB * blk, TB))
                sch.op("sq", ldur, reads=list(usck),
                       writes=["XSTG", ("XSTGu", dch), "INVE", "QN", "SQH"], dma=True)
                def ures(ve, dch=dch):
                    return ve.tensor_add(b.UOS[:, TB * dch:TB * (dch + 1)], b.psA[:, :],
                                         b.XSTG[:, TB * dch:TB * (dch + 1)])
                sch.op("ve", ures, reads=["psA", ("XSTGu", dch), "XSTG"],
                       writes=[("UOS", dch), "UOSln", "TAB"])
                def bcap(ve, blk=blk, dch=dch):
                    return ve.tensor_copy(b.BC[:, 4 * dch + blk:4 * dch + blk + 1],
                                          b.UOS[:, TB * dch + TB - 1:TB * (dch + 1)])
                sch.op("ve", bcap, reads=[("UOS", dch)], writes=[("BC", blk, dch)])
                def stu1(sq, blk=blk, dch=dch):
                    return sq.dma_start(out=d["u1scr"][P * dch:P * (dch + 1), TB * blk:TB * (blk + 1)],
                                        in_=b.UOS[:, TB * dch:TB * (dch + 1)])
                sch.op("sq", stu1, reads=[("UOS", dch)], writes=[("u1", blk, dch)], dma=True)

    emit_l1f_part2(sch, b, d, first, last)

    def fin(sq):
        return sq.nop()
    sch.op("sq", fin, reads=["OUT"])


def _ucols(d, first, col):
    if first or d.get("_fused"):
        return d["uscr"].rearrange("(c p) t -> p c t", p=P)[:, :, col:col + 1]
    if col < NHT * P:
        return d["uhalT"].rearrange("(c p) t -> p c t", p=P)[:, :, col:col + 1]
    return d["uT"].rearrange("(c p) t -> p c t", p=P)[:, :, col - NHT * P:col - NHT * P + 1]


def _qtp_head(b, h, tl):
    off = 64 * (h % 2)
    base = (h // 2) * NQT * P
    return b.QTP[off:off + 64, base + P * (tl - 4): base + P * (tl - 3)]


def _ktp_range(b, h, t0, n):
    off = 64 * (h % 2)
    base = (h // 2) * NKT * P
    return b.KTP[off:off + 64, base + P * t0: base + P * (t0 + n)]


def _ctxt_cols(b, bi):
    return b.CTXT.ap().rearrange("p (c t) -> p c t", c=4)[:, :, P * bi:P * (bi + 1)]


def _ctxt_chunk(b, c):
    return b.CTXT[:, TB * c:TB * (c + 1)]


def emit_l1f_part2(sch, b, d, first, last):
    wqkvk = [("WQKV", c, i) for c in range(NCH) for i in range(2)]
    vlk = [("VL", t_) for t_ in range(NKT)]
    ktpk2 = [("KTP", t_) for t_ in range(NKT)]
    qtpk2 = [("QTP", t_) for t_ in range(NQT)]

    def actb(hb, W):
        if hb < 21:
            return b.KTP[0:W, TB * hb:TB * (hb + 1)]
        return b.QTP[0:W, 0:TB]

    def w2slot(hb, W):
        if hb < 12:
            return b.WQKV[0:W, DM * hb:DM * hb + DM]
        return b.VL[0:W, DM * (hb - 12):DM * (hb - 12) + DM]

    for tb in range(NTB):
        @ngroup(NCH)
        def ldu1(sq, tb=tb):
            return [sq.dma_start(out=b.UOS[:, TB * ch:TB * (ch + 1)],
                                 in_=d["u1scr"][P * ch:P * (ch + 1), TB * tb:TB * (tb + 1)])
                    for ch in range(NCH)]
        sch.op("sq", ldu1, writes=[("UOS", c) for c in range(NCH)] + ["UOSln"], dma=True)
        def h2a(gp, tb=tb):
            out = None
            for ch in range(NCH):
                if ch < 4:
                    out = gp.tensor_copy(b.XSTG[:, TB * ch:TB * (ch + 1)],
                                         b.UOS[:, TB * ch:TB * (ch + 1)])
                else:
                    out = gp.tensor_copy(b.XSTG[:, TB * ch + 1:TB * (ch + 1)],
                                         b.UOS[:, TB * ch:TB * (ch + 1) - 1])
            return out
        sch.op("gp", h2a, reads=[("UOS", c) for c in range(NCH)], writes=["XSTG"])
        def h2b(ve, tb=tb):
            out = None
            for ch in range(4, NCH):
                if tb == 0:
                    out = ve.tensor_copy(b.XSTG[:, TB * ch:TB * ch + 1],
                                         b.UHLAST[:, P * ch + P - 1:P * (ch + 1)])
                else:
                    out = ve.tensor_copy(b.XSTG[:, TB * ch:TB * ch + 1],
                                         b.BC[:, 4 * ch + tb - 1:4 * ch + tb])
            return out
        rl = ([("UHLAST", c) for c in range(4, NCH)] if tb == 0 else
              [("BC", tb - 1, c) for c in range(4, NCH)])
        sch.op("ve", h2b, reads=rl + ["XSTG"], writes=["XSTG"])
        emit_ln_tile_g(sch, b, b.XSTG, "XSTG", b.XH, "XH")

        for hb in range(NHB):
            W = HBW[hb]
            for half in range(2):
                coff = HID * half + (P * hb if hb < 21 else 21 * P)
                sp = (2 * hb + half) % 2
                def ldw1(sq, coff=coff, W=W, sp=sp):
                    return sq.dma_start(
                        out=b.WSTG[:, DM * sp:DM * sp + NCH * W].rearrange("p (c e) -> p c e", c=NCH),
                        in_=d["w1"][:, coff:coff + W].rearrange("(c p) e -> p c e", p=P))
                sch.op("sq", ldw1, writes=[("WSTG", sp)], dma=True)
                def cw1(gp, W=W, sp=sp):
                    return gp.tensor_copy(
                        b.WB1.ap().rearrange("p (c e) -> p c e", c=NCH)[:, :, 0:W],
                        b.WSTG[:, DM * sp:DM * sp + NCH * W].rearrange("p (c e) -> p c e", c=NCH))
                sch.op("gp", cw1, reads=[("WSTG", sp)], writes=["WB1"])
                psk, ps = ("psA", b.psA) if half == 0 else ("psB", b.psB)
                def w1m(pe, ps=ps, W=W):
                    i = None
                    for ch in range(NCH):
                        i = pe.matmul(ps[0:W, :],
                                      b.WB1.ap().rearrange("p (c e) -> p c e", c=NCH)[:, ch, 0:W],
                                      b.XH[:, TB * ch:TB * (ch + 1)],
                                      start=(ch == 0), stop=(ch == NCH - 1))
                    return i
                sch.op("pe", w1m, reads=["WB1", "XH"], writes=[psk])
                if tb == 0:
                    def bm(pe, W=W):
                        i = None
                        for ch in range(NCH):
                            i = pe.matmul(b.pDa[0:1, 0:W], b.LNBC[:, ch:ch + 1],
                                          b.WB1.ap().rearrange("p (c e) -> p c e", c=NCH)[:, ch, 0:W],
                                          start=(ch == 0), stop=(ch == NCH - 1))
                        return i
                    sch.op("pe", bm, reads=["WB1", "LNBC"], writes=["pDa"])
                    def ldb1(sq, coff=coff, W=W):
                        return sq.dma_start(out=b.BSG[0:1, 0:W],
                                            in_=d["b1"].rearrange("(o x) -> o x", o=1)[:, coff:coff + W])
                    sch.op("sq", ldb1, writes=[("BSGr",)], dma=True)
                    def badd(ve, W=W):
                        return ve.tensor_add(b.BSG[0:1, 0:W], b.BSG[0:1, 0:W], b.pDa[0:1, 0:W])
                    sch.op("ve", badd, reads=[("BSGr",), "pDa"], writes=[("BSGr",)])
                    def bcast2(ve, W=W):
                        return ve.tensor_copy(b.BSGB[0:1, 0:P], b.BSG[0:1, 0:P])
                    wrb = [("BSGb",)]
                    if hb == 0 and half == 0:
                        wrb += [("QKG", 1, ti) for ti in range(RGRP)]
                    sch.op("ve", bcast2, reads=[("BSGr",)], writes=wrb)
                    def btr(pe):
                        return pe.transpose(b.psTB[0:P, 0:1], b.BSGB[0:1, 0:P], b.EYEB[0:1, 0:1])
                    sch.op("pe", btr, reads=[("BSGb",), "EYEB"], writes=["psTB"])
                    def bte(ve, half=half, hb=hb, W=W):
                        return ve.tensor_copy(b.B1C[0:W, NHB * half + hb:NHB * half + hb + 1],
                                              b.psTB[0:W, 0:1])
                    sch.op("ve", bte, reads=["psTB"], writes=[("B1C", half, hb)])
                if half == 0:
                    def ev_a(se, hb=hb, W=W):
                        return se.activation(out=b.TMPT[0:W, 0:TB], in_=b.psA[0:W, :],
                                             func=AF.Identity, bias=b.B1C[0:W, hb:hb + 1])
                    sch.op("se", ev_a, reads=["psA", ("B1C", 0, hb)],
                           writes=[("Atile",), ("tmp", 0)])
                else:
                    def ev_g(se, hb=hb, W=W):
                        return se.activation(out=b.TMPT[0:W, TB:2 * TB], in_=b.psB[0:W, :],
                                             func=AF.Gelu, bias=b.B1C[0:W, NHB + hb:NHB + hb + 1])
                    sch.op("se", ev_g, reads=["psB", ("B1C", 1, hb)],
                           writes=[("Gtile",), ("tmp", 1)])
            def glu(ve, hb=hb, W=HBW[hb]):
                return ve.tensor_mul(actb(hb, W), b.TMPT[0:W, 0:TB], b.TMPT[0:W, TB:2 * TB])
            wr2 = [("ACTB", hb)]
            if tb == 0 and hb == 0:
                wr2 += ktpk2 + qtpk2
            sch.op("ve", glu, reads=[("Atile",), ("Gtile",), ("tmp", 0), ("tmp", 1)], writes=wr2)

        for hb in range(NHB):
            W = HBW[hb]
            sp = hb % 2
            def ldw2(sq, hb=hb, W=W, sp=sp):
                return sq.dma_start(out=b.WSTG[0:W, DM * sp:DM * sp + DM],
                                    in_=d["w2"][P * hb:P * hb + W, :])
            sch.op("sq", ldw2, writes=[("WSTG", sp)], dma=True)
            wr = [("WB2", hb)]
            if tb == 0 and hb == 0:
                wr += wqkvk + vlk + [("WO", c) for c in range(4)]
            def cw2(gp, hb=hb, W=W, sp=sp):
                return gp.tensor_copy(w2slot(hb, W), b.WSTG[0:W, DM * sp:DM * sp + DM])
            sch.op("gp", cw2, reads=[("WSTG", sp)], writes=wr)

        for dch in range(NCH):
            def w2m(pe, dch=dch):
                i = None
                for hb in range(NHB):
                    W = HBW[hb]
                    i = pe.matmul(b.pO[:, :], w2slot(hb, W)[0:W, P * dch:P * (dch + 1)],
                                  actb(hb, W),
                                  start=(hb == 0), stop=(hb == NHB - 1))
                return i
            sch.op("pe", w2m, reads=[("WB2", hb) for hb in range(NHB)] +
                   [("ACTB", hb) for hb in range(NHB)],
                   writes=["pO"] + [("pO", p_) for p_ in range(4)])
            if last:
                def ldxr(sq, tb=tb, dch=dch):
                    return sq.dma_start(out=b.XSTG[:, TB * dch:TB * (dch + 1)],
                                        in_=d["xT"][P * dch:P * (dch + 1), TB * tb:TB * (tb + 1)])
                sch.op("sq", ldxr, writes=["XSTG", ("XSTGx", dch)], dma=True)
                def resl(ve, dch=dch):
                    return ve.scalar_tensor_tensor(b.TMPT[:, 0:TB], b.pO[:, :],
                                                   b.B2C[:, dch:dch + 1],
                                                   b.UOS[:, TB * dch:TB * (dch + 1)],
                                                   ALU.add, ALU.add)
                sch.op("ve", resl, reads=["pO", "B2C", ("UOS", dch)], writes=[("tmp", 0)])
                def resl2(ve, dch=dch):
                    return ve.scalar_tensor_tensor(b.UOS[:, TB * dch:TB * (dch + 1)],
                                                   b.TMPT[:, 0:TB], 0.5,
                                                   b.XSTG[:, TB * dch:TB * (dch + 1)],
                                                   ALU.mult, ALU.add)
                sch.op("ve", resl2, reads=[("tmp", 0), ("XSTGx", dch), "XSTG"],
                       writes=[("UOS", dch)])
                def sty(sq, tb=tb, dch=dch):
                    return sq.dma_start(out=d["yT"][P * dch:P * (dch + 1), TB * tb:TB * (tb + 1)],
                                        in_=b.UOS[:, TB * dch:TB * (dch + 1)])
                sch.op("sq", sty, reads=[("UOS", dch)], writes=["OUT"], dma=True)
            else:
                def resn(ve, dch=dch):
                    return ve.scalar_tensor_tensor(b.UOS[:, TB * dch:TB * (dch + 1)], b.pO[:, :],
                                                   b.B2C[:, dch:dch + 1],
                                                   b.UOS[:, TB * dch:TB * (dch + 1)],
                                                   ALU.add, ALU.add)
                sch.op("ve", resn, reads=["pO", "B2C", ("UOS", dch)], writes=[("UOS", dch)])
                def stu2(sq, tb=tb, dch=dch):
                    return sq.dma_start(out=d["uoutT"][P * dch:P * (dch + 1), TB * tb:TB * (tb + 1)],
                                        in_=b.UOS[:, TB * dch:TB * (dch + 1)])
                sch.op("sq", stu2, reads=[("UOS", dch)], writes=["OUT"], dma=True)


def emit_ln_tile_g(sch, b, XS, xsk, dst, dstk):
    pA, pB, pR = b.psA, b.psB, b.pDb
    def cast(se):
        out = None
        for ch in range(NCH):
            xs = XS[:, TB * ch:TB * (ch + 1)]
            se.activation(out=b.XBT[:, TB * ch:TB * (ch + 1)], in_=xs, func=AF.Identity)
            out = se.activation(out=b.SQT[:, 4096 + TB * ch:4096 + TB * (ch + 1)], in_=xs,
                                func=AF.Square)
        return out
    sch.op("se", cast, reads=[xsk], writes=["XBT", "SQT"])
    def stats(pe):
        for ch in range(NCH):
            pe.matmul(pA[0:1, :], b.ONESB[:, 0:1], b.XBT[:, TB * ch:TB * (ch + 1)],
                      start=(ch == 0), stop=(ch == NCH - 1))
        i = None
        for ch in range(NCH):
            i = pe.matmul(pB[0:1, :], b.ONESB[:, 0:1], b.SQT[:, 4096 + TB * ch:4096 + TB * (ch + 1)],
                          start=(ch == 0), stop=(ch == NCH - 1))
        return i
    sch.op("pe", stats, reads=["XBT", "SQT", "ONES"], writes=["psA", "psB"])
    R = b.ROWS
    sch.op("se", lambda se: se.activation(out=R[0:1, 0:TB], in_=pA[0:1, :], func=AF.Identity,
                                          scale=1.0 / DM),
           reads=["psA"], writes=[("rowm",)])
    sch.op("se", lambda se: se.activation(out=R[0:1, 2 * TB:3 * TB], in_=R[0:1, 0:TB], func=AF.Square),
           reads=[("rowm",)], writes=[("rowt",)])
    sch.op("ve", lambda ve: ve.scalar_tensor_tensor(R[0:1, 2 * TB:3 * TB], pB[0:1, :], 1.0 / DM,
                                                    R[0:1, 2 * TB:3 * TB], ALU.mult, ALU.subtract),
           reads=["psB", ("rowt",)], writes=[("rowt",)])
    sch.op("se", lambda se: se.activation(out=R[0:1, 2 * TB:3 * TB], in_=R[0:1, 2 * TB:3 * TB],
                                          func=AF.Ln, bias=cb(b, 1e-5, 1)),
           reads=[("rowt",), "CONSTS"], writes=[("rowt",)])
    sch.op("se", lambda se: se.activation(out=R[0:1, TB:2 * TB], in_=R[0:1, 2 * TB:3 * TB],
                                          func=AF.Exp, scale=-0.5),
           reads=[("rowt",)], writes=[("rowA",)])
    sch.op("ve", lambda ve: ve.scalar_tensor_tensor(R[0:1, 2 * TB:3 * TB], R[0:1, 0:TB], -1.0,
                                                    R[0:1, TB:2 * TB], ALU.mult, ALU.mult),
           reads=[("rowm",), ("rowA",)], writes=[("rowB",), ("rowt",)])
    def bcast(pe):
        pe.matmul(pR[:, 0:TB], b.ONES[0:1, 0:P], R[0:1, TB:2 * TB], start=True, stop=True)
        return pe.matmul(pR[:, TB:2 * TB] if False else b.pDa[:, 0:TB], b.ONES[0:1, 0:P],
                         R[0:1, 2 * TB:3 * TB], start=True, stop=True)
    sch.op("pe", bcast, reads=[("rowA",), ("rowB",), "ONES"], writes=["pDb", "pDa"])
    for ch in range(NCH):
        tp = ch % 2
        def ap1(ve, ch=ch, tp=tp):
            return ve.tensor_mul(b.TMPT[:, TB * tp:TB * (tp + 1)], XS[:, TB * ch:TB * (ch + 1)],
                                 pR[:, 0:TB])
        sch.op("ve", ap1, reads=[xsk, "pDb"], writes=[("tmp", tp)])
        def ap2(ve, ch=ch, tp=tp):
            return ve.tensor_add(b.QKG[:, 0:TB], b.TMPT[:, TB * tp:TB * (tp + 1)], b.pDa[:, 0:TB])
        sch.op("ve", ap2, reads=[("tmp", tp), "pDa"],
               writes=[("lnq",)] + [("QKG", bb, ti) for bb in range(2) for ti in range(RGRP)])
        def ap3(ve, ch=ch):
            return ve.tensor_scalar_mul(dst[:, TB * ch:TB * (ch + 1)], b.QKG[:, 0:TB],
                                        b.LNG[:, ch:ch + 1])
        sch.op("ve", ap3, reads=[("lnq",), "LNG"], writes=[dstk])


def host_inputs_l1f(inp, core, l, first, last, x_cur=None, u_cur=None):
    s, half = core // 2, core % 2
    qoff = half * T
    C = np.ascontiguousarray
    d = {}
    if first:
        d["xT"] = C(x_cur[s, qoff:qoff + T, :].T)
        hal = np.zeros((NHT * P, DM), np.float32)
        if half == 1:
            hal[:] = x_cur[s, qoff - NHT * P:qoff, :]
        d["xhalT"] = C(hal.T)
    else:
        d["uT"] = C(u_cur[s, qoff:qoff + T, :].T)
        hal = np.zeros((NHT * P, DM), np.float32)
        if half == 1:
            hal[:] = u_cur[s, qoff - NHT * P:qoff, :]
        d["uhalT"] = C(hal.T)
        if last:
            d["xT"] = C(x_cur[s, qoff:qoff + T, :].T)
    d["wqkv"] = C(np.asarray(inp["L1_ffn_wqkv"][l]))
    d["wo"] = C(np.asarray(inp["L1_ffn_wo"][l]))
    d["w1"] = C(np.asarray(inp["L1_ffn_w1"][l]))
    d["b1"] = C(np.asarray(inp["L1_ffn_b1"][l]))
    d["w2"] = C(np.asarray(inp["L1_ffn_w2"][l]))
    d["b2"] = C(np.asarray(inp["L1_ffn_b2"][l]))
    d["lng"] = C(np.asarray(inp["L1_ffn_ln_g"][l]))
    d["lnb"] = C(np.asarray(inp["L1_ffn_ln_b"][l]))
    d["qs"] = np.tile(np.asarray(inp["L1_ffn_q_scale"][l]), 8).astype(np.float32)
    d["ks"] = np.tile(np.asarray(inp["L1_ffn_k_scale"][l]), 8).astype(np.float32)
    DH, W = 64, 512
    half_idx = np.arange(0, DH, 2, dtype=np.float64)
    inv_freq = 1.0 / 10000.0 ** (half_idx / DH)
    base = (half_idx + 0.4 * DH) / (1.4 * DH)
    t0 = qoff - 5 * P
    t = np.arange(t0, t0 + NKT * P, dtype=np.float64)
    fr = t[:, None] * inv_freq[None, :]
    fr = np.concatenate([fr, fr], 1)
    power = (t - 4096 // 2) / float(512 // 2)
    sc = np.concatenate([base[None, :] ** power[:, None]] * 2, 1)
    cos, sin = np.cos(fr), np.sin(fr)
    sgn = np.where(np.arange(DH) < 32, -1.0, 1.0)[None, :]
    cs = (cos * sc).astype(np.float32)
    sss = (sin * sc * sgn).astype(np.float32)
    csi = (cos / sc).astype(np.float32)
    ssis = (sin / sc * sgn).astype(np.float32)
    tabs = np.stack([np.tile(x, (1, 8)) for x in (cs, sss, csi, ssis)])
    d["tabs"] = C(tabs)
    hb = np.zeros((P, 1), np.float32)
    if half == 0:
        hb[:, 0] = -1e30
    d["hb"] = hb
    import ml_dtypes
    trib = np.zeros((P, 2 * P), np.float32)
    pi, fi = np.arange(P)[:, None], np.arange(P)[None, :]
    trib[:, 0:P] = np.where(fi > pi, -1e30, 0.0)
    trib[:, P:2 * P] = np.where(fi < pi, -1e30, 0.0)
    d["tribig"] = C(trib.astype(ml_dtypes.bfloat16))
    return d



_BUILD_CACHE = {}


def _get(name, fn):
    if name not in _BUILD_CACHE:
        _BUILD_CACHE[name] = fn()
    return _BUILD_CACHE[name]


def _scatter(res, key):
    full = np.zeros((4, 4096, DM), np.float32)
    for c in range(8):
        s, half = c // 2, c % 2
        full[s, half * T:(half + 1) * T] = np.ascontiguousarray(res.results[c][key]).T
    return full


def _kernel_unfused_impl(**inputs):
    inp = {k: np.asarray(v) for k, v in inputs.items()}
    x = inp["L0_attn_query"].astype(np.float32, copy=True)

    nc = _get("l0ffn", build_l0ffn_launch)
    inp2 = dict(inp); inp2["L0_attn_query"] = x
    res = run_bass_kernel_spmd(nc, [host_inputs_l0ffn(inp2, c) for c in range(8)],
                               core_ids=list(range(8)))
    x = _scatter(res, "yT")

    nc = _get("l0attn", build_l0attn_launch)
    inp2["L0_attn_query"] = x
    res = run_bass_kernel_spmd(nc, [host_inputs_l0attn(inp2, c) for c in range(8)],
                               core_ids=list(range(8)))
    x = _scatter(res, "yT")

    nc = _get("l0ffn", build_l0ffn_launch)
    inp2["L0_attn_query"] = x
    res = run_bass_kernel_spmd(nc, [host_inputs_l0ffn(inp2, c) for c in range(8)],
                               core_ids=list(range(8)))
    x = _scatter(res, "yT")

    ncA = _get("l1fA", lambda: build_l1f_layer_launch(True, False))
    ncB = _get("l1fB", lambda: build_l1f_layer_launch(False, True))
    res = run_bass_kernel_spmd(ncA, [host_inputs_l1f(inp, c, 0, True, False, x_cur=x)
                                     for c in range(8)], core_ids=list(range(8)))
    u1 = _scatter(res, "uoutT")
    res = run_bass_kernel_spmd(ncB, [host_inputs_l1f(inp, c, 1, False, True, x_cur=x, u_cur=u1)
                                     for c in range(8)], core_ids=list(range(8)))
    x = _scatter(res, "yT")

    nc = _get("l1attn", build_l1attn_launch)
    res = run_bass_kernel_spmd(nc, [host_inputs_l1attn(inp, c, x) for c in range(8)],
                               core_ids=list(range(8)))
    x = _scatter(res, "yT")

    res = run_bass_kernel_spmd(ncA, [host_inputs_l1f(inp, c, 0, True, False, x_cur=x)
                                     for c in range(8)], core_ids=list(range(8)))
    u1 = _scatter(res, "uoutT")
    res = run_bass_kernel_spmd(ncB, [host_inputs_l1f(inp, c, 1, False, True, x_cur=x, u_cur=u1)
                                     for c in range(8)], core_ids=list(range(8)))
    x = _scatter(res, "yT")
    return x



def build_fused():
    nc = bass.Bass("TRN2", target_bir_lowering=False, debug=False)
    DI = lambda n, s, dt=F32: nc.dram_tensor(n, s, dt, kind="ExternalInput").ap()
    DS = lambda n, s, dt=F32: nc.dram_tensor(n, s, dt, kind="Internal").ap()
    g = {}
    g["xin"] = DI("xin", [DM, T])
    g["out"] = nc.dram_tensor("out", [DM, T], F32, kind="ExternalOutput").ap()
    g["eye"] = DI("eye", [P, P])
    xd = [DS("xdA", [DM, T]), DS("xdB", [DM, T])]
    g["vscr0"] = DS("vscr0", [32 * P, 1040], BF16)
    g["vscr1"] = DS("vscr1", [32 * P, DM], BF16)
    g["uscr"] = DS("uscr", [DM, T + NHT * P])
    g["u1scr"] = DS("u1scr", [DM, T])
    g["cinx"] = DS("cinx", [DM, T], BF16)
    g["coutx"] = DS("coutx", [2 * DM, T], BF16)
    g["cinu"] = DS("cinu", [DM, NHT * P])
    g["coutu"] = DS("coutu", [2 * DM, NHT * P])
    w = {}
    w["f0"] = {"wrT": DI("f0_wrT", [DM, 16]), "wmhT": DI("f0_wmhT", [DM, DM]),
               "wmgT": DI("f0_wmgT", [DM, DM]), "kfT": DI("f0_kfT", [8, 16, P, 256]),
               "vf": DI("f0_vf", [8, 16, 256, P])}
    w["a0"] = {k: DI("a0_" + k, [DM, DM]) for k in ("qwT", "kwT", "vwT", "owT")}
    w["a0"].update({k: DI("a0_" + k, [DM]) for k in ("qb", "kb", "vb", "ob")})
    w["a1"] = {k: DI("a1_" + k, [DM, DM]) for k in ("wq", "wk", "wv")}
    w["a1"]["biascol"] = DI("a1_biascol", [P, 128])
    w["a1"]["sel"] = nc.dram_tensor("a1_sel", [16, NCH * P], BF16, kind="ExternalInput").ap()
    w["a1"]["tri"] = nc.dram_tensor("a1_tri", [P, P], BF16, kind="ExternalInput").ap()
    for l in range(2):
        w[f"l1{l}"] = {
            "wqkv": DI(f"l1{l}_wqkv", [DM, 3 * TB]), "wo": DI(f"l1{l}_wo", [TB, DM]),
            "w1": DI(f"l1{l}_w1", [DM, 2 * HID]), "b1": DI(f"l1{l}_b1", [2 * HID]),
            "w2": DI(f"l1{l}_w2", [HID, DM]), "b2": DI(f"l1{l}_b2", [DM]),
            "lng": DI(f"l1{l}_lng", [DM]), "lnb": DI(f"l1{l}_lnb", [DM]),
            "qs": DI(f"l1{l}_qs", [TB]), "ks": DI(f"l1{l}_ks", [TB]),
            "tabs": DI(f"l1{l}_tabs", [4, NKT * P, TB]),
        }
    w["l1c"] = {"hb": DI("l1_hb", [P, 1]), "hv": DI("l1_hv", [P, 1]),
                "tribig": nc.dram_tensor("l1_tribig", [P, 2 * P], BF16, kind="ExternalInput").ap()}

    with ExitStack() as ctx:
        sch = Sched(nc, ctx)
        cur = g["xin"]
        seq = [xd[0], xd[1], xd[0], xd[1], xd[0]]

        with ExitStack() as c1:
            b = make_common_bufs(c1, nc, sch)
            reg_consts(b, [1e-5, -10.0])
            emit_setup_consts(sch, b, {"eye": g["eye"]})
            emit_load_x(sch, b, {"xT": cur})
            emit_l0ffn(sch, b, dict(w["f0"]), "f0")
            emit_store_x(sch, b, {"yT": seq[0]})
        sch.barrier(); cur = seq[0]

        with ExitStack() as c2:
            b = make_attn0_bufs(c2, nc)
            reg_consts(b, [1e-5, -10.0])
            emit_setup_consts_noeye(sch, b)
            d2 = dict(w["a0"]); d2.update(xT=cur, yT=seq[1], vscr=g["vscr0"],
                                          cinx=g["cinx"], coutx=g["coutx"])
            emit_l0attn(sch, b, d2, fused=True)
        sch.barrier(); cur = seq[1]

        with ExitStack() as c3:
            b = make_common_bufs(c3, nc, sch)
            reg_consts(b, [1e-5, -10.0])
            emit_setup_consts(sch, b, {"eye": g["eye"]})
            emit_load_x(sch, b, {"xT": cur})
            emit_l0ffn(sch, b, dict(w["f0"]), "f0")
            emit_store_x(sch, b, {"yT": seq[2]})
        sch.barrier(); cur = seq[2]

        for l, last in ((0, False), (1, False)):
            with ExitStack() as c4:
                b = make_l1f_bufs(c4, nc)
                reg_consts(b, [1e-5, -10.0, 1e-30])
                emit_setup_consts_noeye(sch, b)
                d4 = dict(w[f"l1{l}"]); d4.update(w["l1c"])
                d4.update(xT=cur, uscr=g["uscr"], u1scr=g["u1scr"], _fused=True,
                          cinu=g["cinu"], coutu=g["coutu"],
                          uoutT=g["uscr"][:, NHT * P:NHT * P + T])
                if l == 1:
                    d4["yT"] = seq[3]
                emit_l1f_layer(sch, b, d4, first=(l == 0), last=(l == 1), fused=True)
            sch.barrier()
        cur = seq[3]

        with ExitStack() as c6:
            b = make_attn1_bufs(c6, nc)
            reg_consts(b, [1e-5, -10.0])
            emit_setup_consts_noeye(sch, b)
            d6 = dict(w["a1"]); d6.update(xT=cur, yT=seq[4], vscr=g["vscr1"],
                                          cinx=g["cinx"], coutx=g["coutx"], eye=g["eye"])
            emit_l1attn(sch, b, d6, fused=True)
        sch.barrier(); cur = seq[4]

        for l, last in ((0, False), (1, True)):
            with ExitStack() as c7:
                b = make_l1f_bufs(c7, nc)
                reg_consts(b, [1e-5, -10.0, 1e-30])
                emit_setup_consts_noeye(sch, b)
                d7 = dict(w[f"l1{l}"]); d7.update(w["l1c"])
                d7.update(xT=cur, uscr=g["uscr"], u1scr=g["u1scr"], _fused=True,
                          cinu=g["cinu"], coutu=g["coutu"],
                          uoutT=g["uscr"][:, NHT * P:NHT * P + T])
                if l == 1:
                    d7["yT"] = g["out"]
                emit_l1f_layer(sch, b, d7, first=(l == 0), last=(l == 1), fused=True)
            sch.barrier()

        with nc.Block() as block:
            sch.build(block)
    return nc


def host_inputs_fused(inp, core, dummyx):
    s, half = core // 2, core % 2
    C = np.ascontiguousarray
    m = {}
    f = host_inputs_l0ffn(inp, core)
    m["xin"] = f.pop("xT")
    m["eye"] = f.pop("eye")
    for k, v in f.items():
        m["f0_" + k] = v
    a0 = host_inputs_l0attn(inp, core)
    a0.pop("xT"); a0.pop("xTf")
    for k, v in a0.items():
        m["a0_" + k] = v
    a1 = host_inputs_l1attn(inp, core, dummyx)
    a1.pop("xT"); a1.pop("xTf"); a1.pop("eye")
    for k, v in a1.items():
        m["a1_" + k] = v
    for l in range(2):
        d = host_inputs_l1f(inp, core, l, True, False, x_cur=dummyx)
        d.pop("xT"); d.pop("xhalT")
        hb = d.pop("hb"); d.pop("tribig") if False else None
        trib = d.pop("tribig")
        for k, v in d.items():
            m[f"l1{l}_" + k] = v
        if l == 0:
            m["l1_hb"] = hb
            m["l1_tribig"] = trib
    m["l1_hv"] = np.full((P, 1), 0.0 if half == 0 else 1.0, np.float32)
    return m


def kernel(**inputs):
    inp = {k: np.asarray(v) for k, v in inputs.items()}
    nc = _get("fused", build_fused)
    dummyx = np.zeros((4, 4096, DM), np.float32)
    in_maps = []
    for c in range(8):
        m = host_inputs_fused(inp, c, dummyx)
        res_x = inp["L0_attn_query"]
        s, half = c // 2, c % 2
        m["xin"] = np.ascontiguousarray(res_x[s, half * T:(half + 1) * T, :].T.astype(np.float32))
        in_maps.append(m)
    res = run_bass_kernel_spmd(nc, in_maps, core_ids=list(range(8)))
    return _scatter(res, "out")


def kernel_unfused(**inputs):
    return _kernel_unfused_impl(**inputs)
